# version 02 (final)
# baseline (speedup 1.0000x reference)
import numpy as np
import ml_dtypes
from contextlib import ExitStack
import concourse.bass as bass
import concourse.mybir as mybir
from concourse.bass_utils import run_bass_kernel_spmd
from concourse.bass_core_rust import IndirectOffsetOnAxis

F32 = mybir.dt.float32
BF16 = mybir.dt.bfloat16
I32 = mybir.dt.int32
AF = mybir.ActivationFunctionType
ALU = mybir.AluOpType
AX = mybir.AxisListType
NPBF = ml_dtypes.bfloat16

ENGINES = ("sync", "tensor", "vector", "scalar", "gpsimd")
NSEM = 8
NEG = -1e30
GROUPS = [[0, 1, 2, 3], [4, 5, 6, 7]]

RACE_DET = True


def mknc():
    return bass.Bass("TRN2", target_bir_lowering=False,
                     detect_race_conditions=RACE_DET)


class Prog:

    def __init__(self):
        self.phases = [[]]

    def op(self, eng, fn, dma=False, coll=False):
        assert eng in ENGINES
        kind = "coll" if coll else ("dma" if dma else "comp")
        self.phases[-1].append((eng, fn, kind))

    def barrier(self):
        if self.phases[-1]:
            self.phases.append([])

    def emit(self, nc, sems=None):
        self.barrier()
        self.op("sync", lambda e: e.nop())
        phases = [p for p in self.phases if p]
        n = len(phases)
        NC, ND = 4, 4
        csems = [nc.alloc_semaphore(f"csem{i}") for i in range(NC)]
        dsems = [nc.alloc_semaphore(f"dsem{i}") for i in range(ND)]
        ncoll = sum(1 for p in phases for (_, _, k) in p if k == "coll")
        csems_coll = [nc.alloc_semaphore(f"ksem{i}") for i in range(ncoll)]

        ssem = nc.alloc_semaphore("ssem")
        ccum = [0] * NC
        dcum = [0] * ND
        scum = 0
        phase_end = []
        coll_idx = 0
        coll_sem_of_phase = [None] * n
        sw_of_phase = [0] * n
        for k, p in enumerate(phases):
            engs = set()
            dtot = 0
            stot = 0
            cs, ds = csems[k % NC], dsems[k % ND]
            for eng, fn, kind in p:
                if kind == "dma":
                    if eng == "gpsimd":
                        stot += 16
                    else:
                        dtot += 16
                elif kind == "comp":
                    engs.add(eng)
                else:
                    coll_sem_of_phase[k] = csems_coll[coll_idx]
                    coll_idx += 1
            ends = []
            if engs:
                ccum[k % NC] += len(engs)
                ends.append((cs, ccum[k % NC]))
            if dtot:
                dcum[k % ND] += dtot
                ends.append((ds, dcum[k % ND]))
            if stot:
                scum += stot
                ends.append((ssem, scum))
            sw_of_phase[k] = scum
            if coll_sem_of_phase[k] is not None:
                ends.append((coll_sem_of_phase[k], 1))
            phase_end.append(ends)

        with nc.Block() as block:
            def make_section(eng_name):
                def section(eng):
                    for k, p in enumerate(phases):
                        my_ops = [(fn, kind) for (e, fn, kind) in p if e == eng_name]
                        if not my_ops:
                            continue
                        if k > 0:
                            for sem, val in phase_end[k - 1]:
                                eng.wait_ge(sem, val)
                        cs, ds = csems[k % NC], dsems[k % ND]
                        last_compute = None
                        for fn, kind in my_ops:
                            inst = fn(eng)
                            if kind == "dma":
                                inst.then_inc(ssem if eng_name == "gpsimd" else ds, 16)
                            elif kind == "coll":
                                inst.then_inc(coll_sem_of_phase[k], 1)
                            else:
                                last_compute = inst
                        if last_compute is not None:
                            last_compute.then_inc(cs, 1)
                return section
            for e in ENGINES:
                getattr(block, e)(make_section(e))


def dma(P, dst, src, eng="sync"):
    P.op(eng, lambda e, d=dst, s=src: e.dma_start(out=d, in_=s), dma=True)


def dma_nc(P, dst, src, eng="sync"):
    P.op(eng, lambda e, d=dst, s=src: e.dma_start(
        out=d, in_=s, allow_slow_non_contiguous=True), dma=True)


def dma_cast(P, dst, src):
    P.op("gpsimd", lambda e, d=dst, s=src: e.dma_start(out=d, in_=s), dma=True)


def idma(P, dst, src, idx_ap, eoff=0):
    P.op("gpsimd", lambda e, d=dst, s=src, i=idx_ap, eo=eoff: e.indirect_dma_start(
        out=d, out_offset=None, in_=s, in_offset=IndirectOffsetOnAxis(i, 0),
        element_offset=eo), dma=True)


def mm(P, out, lhsT, rhs, start=True, stop=True):
    P.op("tensor", lambda e, o=out, l=lhsT, r=rhs, st=start, sp=stop:
         e.matmul(o, l, r, start=st, stop=sp))


def tr(P, out_psum, in_, ident):
    P.op("tensor", lambda e, o=out_psum, i=in_, d=ident: e.transpose(o, i, d))


def ew(P, eng, name, *args, **kw):
    P.op(eng, lambda e, n=name, a=args, k=kw: getattr(e, n)(*a, **k))


def act(P, out, in_, func, bias=0.0, scale=1.0, accum_out=None):
    kw = {}
    if accum_out is not None:
        kw["accum_out"] = accum_out
    P.op("scalar", lambda e, o=out, i=in_, f=func, b=bias, s=scale, k=kw:
         e.activation(o, i, f, bias=b, scale=s, **k))


def coll(P, kind, op, ins, outs, rg=GROUPS):
    P.op("gpsimd", lambda e, k=kind, o=op, i=ins, u=outs, r=rg:
         e.collective_compute(k, o, replica_groups=r, ins=i, outs=u), coll=True)


def make_consts():
    c = {}
    ii = np.eye(128, dtype=np.float32)
    c["identf"] = ii
    c["identb"] = ii.astype(NPBF)
    c["ones1f"] = np.ones((1, 128), np.float32)
    c["onescf"] = np.ones((128, 1), np.float32)
    c["onescb"] = np.ones((128, 1), np.float32).astype(NPBF)
    b = np.zeros((8, 1024), np.float32)
    for h in range(8):
        b[h, h * 128:(h + 1) * 128] = 1.0
    c["bsel8"] = b
    t = np.arange(128)
    c["msk_su"] = np.where(t[:, None] <= t[None, :], 0.0, NEG).astype(np.float32)
    c["msk_sl"] = np.where(t[:, None] >= t[None, :], 0.0, NEG).astype(np.float32)
    c["msk_sus"] = np.where(t[:, None] < t[None, :], 0.0, NEG).astype(np.float32)
    c["msk_sls"] = np.where(t[:, None] > t[None, :], 0.0, NEG).astype(np.float32)
    c["triu_onesb"] = np.triu(np.ones((128, 128), np.float32)).astype(NPBF)
    s = np.zeros((128, 1), np.float32); s[127, 0] = 1.0
    c["sel127"] = s
    hs = np.zeros((128, 2), np.float32)
    hs[0:64, 0] = 1.0; hs[64:128, 1] = 1.0
    c["hs2b"] = hs.astype(NPBF)
    hc = np.zeros((128, 128), np.float32)
    for kc in range(8):
        hc[0:64, kc * 16 + 2 * kc] = 1.0
        hc[64:128, kc * 16 + 2 * kc + 1] = 1.0
    c["hsel16c"] = hc
    c["e8row"] = np.broadcast_to(np.arange(8, dtype=np.float32), (128, 8)).copy()
    b16 = np.zeros((16, 2048), np.float32)
    for h in range(16):
        b16[h, h * 128:(h + 1) * 128] = 1.0
    c["bsel16"] = b16
    c["antidiag"] = np.eye(128, dtype=np.float32)[::-1].copy()
    return c


CONST_DT = {"identf": F32, "identb": BF16, "ones1f": F32, "onescf": F32, "onescb": BF16, "bsel8": F32,
            "msk_su": F32, "msk_sl": F32, "msk_sus": F32, "msk_sls": F32,
            "triu_onesb": BF16, "sel127": F32, "e8row": F32, "bsel16": F32, "antidiag": F32, "hs2b": BF16, "hsel16c": F32}


def const_inputs():
    return {"c_" + k: v for k, v in make_consts().items()}


CONST_VALS = [1e-5, 1.0, 1e-6, 1e-12, 2.0]


class Consts:
    def __init__(self, nc, P):
        vals = make_consts()
        for k, dt in CONST_DT.items():
            d = nc.dram_tensor("c_" + k, list(vals[k].shape), dt, kind="ExternalInput").ap()
            t = nc.alloc_sbuf_tensor("k_" + k, list(vals[k].shape), dt).ap()
            setattr(self, k, t)
            dma(P, t[:, :], d[:, :])
        for i, v in enumerate(CONST_VALS):
            t = nc.alloc_sbuf_tensor(f"k_cv{i}", [128, 1], F32).ap()
            ew(P, "gpsimd", "memset", t[:, :], v)
            nc.const_aps.register(F32, v, t)


def sball(nc, stack, name, shape, dt):
    return stack.enter_context(nc.sbuf_tensor(name, shape, dt)).ap()


def fmv(dram_ap, kc, c0, c1):
    return dram_ap.rearrange("(c p) t -> c p t", p=128)[kc, :, c0:c1]


def alloc_psums(nc):
    return [nc.alloc_psum_tensor(f"ps{i}", [128, 512], F32).ap() for i in range(7)] + \
           [nc.alloc_psum_tensor("psT", [128, 1024], BF16).ap()]


def alloc_sems(nc):
    return None


def emit_ln(P, C, ps, load_chunk, xn_out, subtiles, xt, scr, scr2, murow, rstdrow,
            nd=1024, eps=1e-5):
    pA, pB, pC, pD = ps[0], ps[1], ps[2], ps[3]
    for t0, w in subtiles:
        for kc in range(8):
            load_chunk(kc, t0, w, xt[:, kc, 0:w])
        P.barrier()
        for kc in range(8):
            mm(P, pA[0:1, 0:w], C.onescf[:, 0:1], xt[:, kc, 0:w],
               start=(kc == 0), stop=(kc == 7))
        for kc in range(8):
            s = scr if kc % 2 == 0 else scr2
            act(P, s[:, 0:w], xt[:, kc, 0:w], AF.Square)
            P.barrier()
            mm(P, pB[0:1, 0:w], C.onescf[:, 0:1], s[:, 0:w],
               start=(kc == 0), stop=(kc == 7))
        P.barrier()
        ew(P, "vector", "tensor_scalar_mul", murow[0:1, 0:w], pA[0:1, 0:w], 1.0 / nd)
        ew(P, "vector", "tensor_scalar_mul", rstdrow[0:1, 0:w], pB[0:1, 0:w], 1.0 / nd)
        P.barrier()
        ew(P, "vector", "tensor_mul", scr[0:1, 0:w], murow[0:1, 0:w], murow[0:1, 0:w])
        P.barrier()
        ew(P, "vector", "tensor_sub", rstdrow[0:1, 0:w], rstdrow[0:1, 0:w], scr[0:1, 0:w])
        P.barrier()
        act(P, scr2[0:1, 0:w], rstdrow[0:1, 0:w], AF.Ln, bias=eps)
        P.barrier()
        act(P, rstdrow[0:1, 0:w], scr2[0:1, 0:w], AF.Exp, scale=-0.5)
        P.barrier()
        mm(P, pC[:, 0:w], C.ones1f[0:1, 0:128], murow[0:1, 0:w])
        mm(P, pD[:, 0:w], C.ones1f[0:1, 0:128], rstdrow[0:1, 0:w])
        P.barrier()
        for kc in range(8):
            ew(P, "vector", "tensor_sub", xt[:, kc, 0:w], xt[:, kc, 0:w], pC[:, 0:w])
        P.barrier()
        for kc in range(8):
            ew(P, "vector", "tensor_mul", xn_out(kc, t0, w), xt[:, kc, 0:w], pD[:, 0:w])
        P.barrier()



def _dump_chunks(P, xo, src_fn, nch, c0, c1):
    for kc in range(nch):
        dma_cast(P, fmv(xo, kc, 0, c1 - c0), src_fn(kc))

def build_mamba(direction, upto=None):
    fwd = direction == "fwd"
    nc = mknc()

    xh = nc.dram_tensor("xh", [1024, 2054], F32, kind="ExternalInput").ap()
    xresd = nc.dram_tensor("xres", [1024, 2048], F32, kind="ExternalInput").ap()
    wiT = nc.dram_tensor("wiT", [1024, 4384], F32, kind="ExternalInput").ap()
    convw = nc.dram_tensor("convw", [128, 72], F32, kind="ExternalInput").ap()
    convb = nc.dram_tensor("convb", [128, 18], F32, kind="ExternalInput").ap()
    dtbd = nc.dram_tensor("dtb", [32, 1], F32, kind="ExternalInput").ap()
    alogd = nc.dram_tensor("alog8", [8, 1], F32, kind="ExternalInput").ap()
    dvecd = nc.dram_tensor("dvecbc", [128, 4], F32, kind="ExternalInput").ap()
    woutd = nc.dram_tensor("wout512", [1024, 512], F32, kind="ExternalInput").ap()
    projTd = nc.dram_tensor("projTd", [1024, 1024], F32, kind="ExternalInput").ap()
    normwd = nc.dram_tensor("normw4", [128, 4], F32, kind="ExternalInput").ap()
    pbd = nc.dram_tensor("pb_r", [128, 8], F32, kind="ExternalInput").ap()
    nwgd = nc.dram_tensor("nwg_r", [128, 8], F32, kind="ExternalInput").ap()
    hidxd = nc.dram_tensor("hidx_d", [128, 8], I32, kind="ExternalInput").ap()
    didxd = nc.dram_tensor("didx_d", [8, 1], I32, kind="ExternalInput").ap()
    xo = nc.dram_tensor("xo", [1024, 2048], F32, kind="ExternalOutput").ap()

    cc_in = nc.dram_tensor("cc_in", [4352, 2048], BF16)
    cc_out = nc.dram_tensor("cc_out", [4, 4352, 2048], BF16)
    ccd_in = nc.dram_tensor("ccd_in", [32, 2048], F32)
    ccd_out = nc.dram_tensor("ccd_out", [4, 32, 2048], F32)
    ccq_in = nc.dram_tensor("ccq_in", [1, 8192], F32)
    ccq_out = nc.dram_tensor("ccq_out", [1, 8192], F32)
    ccr_in = nc.dram_tensor("ccr_in", [4096, 2048], F32)
    ccr_out = nc.dram_tensor("ccr_out", [1024, 2048], F32)
    gsp = nc.dram_tensor("gsp", [4, 128, 8192], F32).ap()
    xnsp = nc.dram_tensor("xnsp", [1024, 2048], BF16).ap()

    P = Prog()
    sems = alloc_sems(nc)
    ps = alloc_psums(nc)
    pT = ps[7]
    C = Consts(nc, P)

    cw = nc.alloc_sbuf_tensor("cw", [128, 72], F32).ap()
    cb = nc.alloc_sbuf_tensor("cb", [128, 18], F32).ap()
    dtb = nc.alloc_sbuf_tensor("dtbt", [32, 1], F32).ap()
    a8 = nc.alloc_sbuf_tensor("a8", [8, 1], F32).ap()
    a8b = nc.alloc_sbuf_tensor("a8b", [8, 1], F32).ap()
    dvt = nc.alloc_sbuf_tensor("dvt", [128, 4], F32).ap()
    nwt = nc.alloc_sbuf_tensor("nwt", [128, 4], F32).ap()
    pbt = nc.alloc_sbuf_tensor("pbt", [128, 8], F32).ap()
    nwg = nc.alloc_sbuf_tensor("nwg", [128, 8], F32).ap()
    wfm = nc.alloc_sbuf_tensor("wfm", [128, 4, 1024], BF16).ap()
    onesr = nc.alloc_sbuf_tensor("onesr", [8, 512], F32).ap()
    hidx = nc.alloc_sbuf_tensor("hidx", [128, 8], I32).ap()
    didx = nc.alloc_sbuf_tensor("didx", [8, 1], I32).ap()

    dma(P, cw[:, :], convw[:, :])
    dma(P, cb[:, :], convb[:, :])
    dma(P, dtb[:, :], dtbd[:, :])
    dma(P, a8[:, :], alogd[:, :])
    dma(P, dvt[:, :], dvecd[:, :])
    dma(P, nwt[:, :], normwd[:, :])
    dma(P, pbt[:, :], pbd[:, :])
    dma(P, nwg[:, :], nwgd[:, :])
    dma(P, hidx[:, :], hidxd[:, :])
    dma(P, didx[:, :], didxd[:, :])
    ew(P, "gpsimd", "memset", onesr[:, :], 1.0)
    P.barrier()
    act(P, dtb[:, 0:1] if False else a8b[:, :], a8[:, :], AF.Exp)
    P.barrier()
    ew(P, "vector", "tensor_scalar_mul", a8[:, :], a8b[:, :], -1.0)
    P.barrier()

    with ExitStack() as g0:
        prT_bf = sball(nc, g0, "prT_bf", [128, 8, 1024], BF16)
        wo_bf = sball(nc, g0, "wo_bf", [128, 8, 128], BF16)
        for kc in range(8):
            dma_cast(P, prT_bf[:, kc, :],
                     projTd.rearrange("(c p) o -> c p o", p=128)[kc, :, :])
        P.barrier()
        for jt in range(4):
            for mc in range(8):
                dma_cast(P, wo_bf[:, mc, :],
                         woutd.rearrange("(c p) j -> c p j", p=128)[mc, :, jt * 128:(jt + 1) * 128])
            P.barrier()
            for half in range(2):
                for mc in range(8):
                    mm(P, ps[half][:, :], wo_bf[:, mc, :],
                       prT_bf[:, mc, half * 512:(half + 1) * 512],
                       start=(mc == 0), stop=(mc == 7))
            P.barrier()
            for half in range(2):
                ew(P, "vector", "tensor_scalar_mul",
                   wfm[:, jt, half * 512:(half + 1) * 512], ps[half][:, :],
                   nwt[:, jt:jt + 1])
            P.barrier()

    with ExitStack() as g1:
        xn_bf = sball(nc, g1, "xn_bf", [128, 8, 2054], BF16)
        rawbc = sball(nc, g1, "rawbc", [128, 18, 2054], BF16)
        wi_bf = sball(nc, g1, "wi_bf", [128, 8, 512], BF16)
        xt = sball(nc, g1, "xt", [128, 8, 512], F32)
        scr = sball(nc, g1, "scr", [128, 512], F32)
        scr2 = sball(nc, g1, "scr2", [128, 512], F32)
        murow = sball(nc, g1, "murow", [1, 512], F32)
        rstdrow = sball(nc, g1, "rstdrow", [1, 512], F32)
        dts1 = sball(nc, g1, "dts1", [32, 512], F32)
        dts2 = sball(nc, g1, "dts2", [32, 512], F32)
        dts3 = sball(nc, g1, "dts3", [32, 512], F32)
        cvb = sball(nc, g1, "cvb", [128, 2, 512], BF16)
        zstg = sball(nc, g1, "zstg", [128, 2, 512], BF16)

        subs = [(0, 512), (512, 512), (1024, 512), (1536, 512), (2048, 6)]
        pA, pB, pC, pD = ps[0], ps[1], ps[2], ps[3]
        for t0, w in subs:
            for kc in range(8):
                dma(P, xt[:, kc, 0:w], fmv(xh, kc, t0, t0 + w))
            P.barrier()
            for kc in range(8):
                mm(P, pA[0:1, 0:w], C.onescf[:, 0:1], xt[:, kc, 0:w],
                   start=(kc == 0), stop=(kc == 7))
            for kc in range(8):
                s = scr if kc % 2 == 0 else scr2
                act(P, s[:, 0:w], xt[:, kc, 0:w], AF.Square)
                P.barrier()
                mm(P, pB[0:1, 0:w], C.onescf[:, 0:1], s[:, 0:w],
                   start=(kc == 0), stop=(kc == 7))
            P.barrier()
            ew(P, "vector", "tensor_scalar_mul", murow[0:1, 0:w], pA[0:1, 0:w], 1.0 / 1024)
            ew(P, "vector", "tensor_scalar_mul", rstdrow[0:1, 0:w], pB[0:1, 0:w], 1.0 / 1024)
            P.barrier()
            ew(P, "vector", "tensor_mul", scr[0:1, 0:w], murow[0:1, 0:w], murow[0:1, 0:w])
            P.barrier()
            ew(P, "vector", "tensor_sub", rstdrow[0:1, 0:w], rstdrow[0:1, 0:w], scr[0:1, 0:w])
            P.barrier()
            act(P, scr2[0:1, 0:w], rstdrow[0:1, 0:w], AF.Ln, bias=1e-5)
            P.barrier()
            act(P, rstdrow[0:1, 0:w], scr2[0:1, 0:w], AF.Exp, scale=-0.5)
            P.barrier()
            mm(P, pC[:, 0:w], C.ones1f[0:1, 0:128], murow[0:1, 0:w])
            mm(P, pD[:, 0:w], C.ones1f[0:1, 0:128], rstdrow[0:1, 0:w])
            P.barrier()
            for kc in range(8):
                ew(P, "vector", "tensor_sub", xt[:, kc, 0:w], xt[:, kc, 0:w], pC[:, 0:w])
            P.barrier()
            for kc in range(8):
                ew(P, "vector", "tensor_mul", xt[:, kc, 0:w], xt[:, kc, 0:w], pD[:, 0:w])
            P.barrier()
            if fwd:
                lo = max(t0, 3); hi = min(t0 + w, 2051)
                if lo < hi:
                    for kc in range(8):
                        dma_cast(P, fmv(xnsp, kc, lo - 3, hi - 3), xt[:, kc, lo - t0:hi - t0])
            for kc in range(8):
                s = scr if kc % 2 == 0 else scr2
                act(P, s[:, 0:w], xt[:, kc, 0:w], AF.Square)
                P.barrier()
                mm(P, pB[0:1, 0:w], C.onescf[:, 0:1], s[:, 0:w],
                   start=(kc == 0), stop=(kc == 7))
            P.barrier()
            ew(P, "vector", "tensor_scalar_mul", rstdrow[0:1, 0:w], pB[0:1, 0:w], 1.0 / 1024)
            P.barrier()
            act(P, scr2[0:1, 0:w], rstdrow[0:1, 0:w], AF.Ln, bias=1e-5)
            P.barrier()
            act(P, rstdrow[0:1, 0:w], scr2[0:1, 0:w], AF.Exp, scale=-0.5)
            P.barrier()
            mm(P, pD[:, 0:w], C.ones1f[0:1, 0:128], rstdrow[0:1, 0:w])
            P.barrier()
            for kc in range(8):
                ew(P, "vector", "tensor_mul", scr[:, 0:w], xt[:, kc, 0:w], pD[:, 0:w])
                P.barrier()
                ew(P, "vector", "tensor_scalar_mul", xn_bf[:, kc, t0:t0 + w], scr[:, 0:w],
                   nwg[:, kc:kc + 1])
            P.barrier()

        if upto == "ln":
            _dump_chunks(P, xo, lambda kc: xn_bf[:, kc, 3:2051], 8, 0, 2048)
            P.emit(nc, sems)
            return nc

        for og in range(9):
            o0 = og * 512
            ow = min(512, 4384 - o0)
            noc = (ow + 127) // 128
            for kc in range(8):
                dma_cast(P, wi_bf[:, kc, 0:ow],
                         wiT.rearrange("(c p) o -> c p o", p=128)[kc, :, o0:o0 + ow])
            P.barrier()
            for oc in range(noc):
                oa = o0 + oc * 128
                ocw = min(128, ow - oc * 128)
                is_z = oa < 2048
                is_dt = oa >= 4352
                cts = [(3 + i * 512, 512) for i in range(4)] if (is_z or is_dt) \
                    else [(0, 512), (512, 512), (1024, 512), (1536, 512), (2048, 6)]
                for ci, (tc0, tw) in enumerate(cts):
                    bank = ps[ci % 2]
                    for kc in range(8):
                        mm(P, bank[0:ocw, 0:tw], wi_bf[:, kc, oc * 128:oc * 128 + ocw],
                           xn_bf[:, kc, tc0:tc0 + tw], start=(kc == 0), stop=(kc == 7))
                    P.barrier()
                    if is_z:
                        zc = oa // 128
                        stg = zstg[:, ci % 2, 0:tw]
                        ew(P, "vector", "tensor_copy", stg, bank[:, 0:tw])
                        P.barrier()
                        dma(P, cc_in.ap().rearrange("(c p) t -> c p t", p=128)
                            [16 + zc, :, tc0 - 3:tc0 - 3 + tw], stg)
                    elif is_dt:
                        ew(P, "vector", "tensor_scalar_add", dts1[:, 0:tw], bank[0:32, 0:tw], dtb[:, 0:1])
                        P.barrier()
                        ew(P, "vector", "tensor_relu", dts2[:, 0:tw], dts1[:, 0:tw])
                        P.barrier()
                        act(P, dts3[:, 0:tw], dts1[:, 0:tw], AF.Abs)
                        P.barrier()
                        act(P, dts1[:, 0:tw], dts3[:, 0:tw], AF.Exp, scale=-1.0)
                        P.barrier()
                        act(P, dts3[:, 0:tw], dts1[:, 0:tw], AF.Ln, bias=1.0)
                        P.barrier()
                        ew(P, "vector", "tensor_add", dts1[:, 0:tw], dts3[:, 0:tw], dts2[:, 0:tw])
                        P.barrier()
                        dma(P, ccd_in.ap()[:, tc0 - 3:tc0 - 3 + tw], dts1[:, 0:tw])
                    else:
                        bcc = (oa - 2048) // 128
                        ew(P, "vector", "tensor_copy", rawbc[0:ocw, bcc, tc0:tc0 + tw],
                           bank[0:ocw, 0:tw])
                    P.barrier()

        if upto == "win":
            _dump_chunks(P, xo, lambda kc: rawbc[:, kc, 3:2051], 8, 0, 2048)
            P.emit(nc, sems)
            return nc

        for st in range(4):
            t0 = st * 512
            for bc in range(18):
                sbuf = scr if bc % 2 == 0 else scr2
                eng = "vector"
                for k in range(4):
                    rc = (t0 + k) if fwd else (t0 + 6 - k)
                    if k == 0:
                        ew(P, eng, "tensor_scalar_mul", sbuf[:, :],
                           rawbc[:, bc, rc:rc + 512], cw[:, bc * 4 + k:bc * 4 + k + 1])
                    else:
                        ew(P, eng, "scalar_tensor_tensor", sbuf[:, :],
                           rawbc[:, bc, rc:rc + 512], cw[:, bc * 4 + k:bc * 4 + k + 1],
                           sbuf[:, :], ALU.mult, ALU.add)
                P.barrier()
                act(P, cvb[:, bc % 2, :], sbuf[:, :], AF.Silu, bias=cb[:, bc:bc + 1])
                P.barrier()
                if bc < 16:
                    dst = cc_in.ap().rearrange("(c p) t -> c p t", p=128)[bc, :, t0:t0 + 512]
                else:
                    dst = cc_in.ap().rearrange("(c p) t -> c p t", p=128)[16 + 16 + bc - 16, :, t0:t0 + 512]
                dma(P, dst, cvb[:, bc % 2, :])
                P.barrier()

        if upto == "conv":
            for kc in range(8):
                dma_cast(P, xt[:, kc, 0:512], cc_in.ap().rearrange("(c p) t -> c p t", p=128)[kc, :, 0:512])
            P.barrier()
            for kc in range(8):
                dma(P, fmv(xo, kc, 0, 512), xt[:, kc, 0:512])
            P.emit(nc, sems)
            return nc

    coll(P, "AllGather", ALU.bypass, [cc_in.ap().opt()], [cc_out.ap().opt()])
    P.barrier()
    coll(P, "AllGather", ALU.bypass, [ccd_in.ap().opt()], [ccd_out.ap().opt()])
    P.barrier()

    if upto == "cc":
        with ExitStack() as gd:
            dbg = sball(nc, gd, "dbg", [128, 512], F32)
            for kc in range(8):
                dma_cast(P, dbg[:, :], cc_out.ap()[0, kc * 128:(kc + 1) * 128, 0:512])
                P.barrier()
                dma(P, fmv(xo, kc, 0, 512), dbg[:, :])
                P.barrier()
        P.emit(nc, sems)
        return nc

    with ExitStack() as g2:
        gxs = sball(nc, g2, "gxs", [128, 4, 2048], BF16)
        gzs = sball(nc, g2, "gzs", [128, 4, 2048], BF16)
        gzsil = sball(nc, g2, "gzsil", [128, 4, 2048], BF16)
        gbc = sball(nc, g2, "gbc", [128, 2, 2048], BF16)
        gdt = sball(nc, g2, "gdt", [8, 2048], F32)
        lgA = sball(nc, g2, "lgA", [8, 512], F32)
        lgs = sball(nc, g2, "lgs", [8, 512], F32)
        nlg = sball(nc, g2, "nlg", [8, 512], F32)
        eglr = sball(nc, g2, "eglr", [8, 512], F32)
        egr2 = sball(nc, g2, "egr2", [8, 512], F32)
        egtmp = sball(nc, g2, "egtmp", [8, 512], F32)
        srow = sball(nc, g2, "srow", [8, 1], F32)
        chat8 = sball(nc, g2, "chat8", [128, 8, 512], F32)
        xhat4 = sball(nc, g2, "xhat4", [128, 4, 512], BF16)
        ghat_c = sball(nc, g2, "ghat_c", [128, 8, 128], BF16)
        xhtok_c = sball(nc, g2, "xhtok_c", [128, 4, 128], BF16)
        xgtok_c = sball(nc, g2, "xgtok_c", [128, 8, 64], F32)
        btok_f = sball(nc, g2, "btok_f", [128, 128], F32)
        gsb = sball(nc, g2, "gsb", [128, 128], F32)
        tcole = sball(nc, g2, "tcole", [128, 8], F32)
        dsb = sball(nc, g2, "dsb", [128, 2, 512], F32)
        Hn = sball(nc, g2, "Hn", [128, 8, 64], F32)
        hscl = sball(nc, g2, "hscl", [128, 2, 64], F32)
        ybuf = sball(nc, g2, "ybuf", [128, 512], F32)
        gbuf = sball(nc, g2, "gbuf", [128, 512], F32)
        sqs = sball(nc, g2, "sqs", [128, 512], F32)
        qrow = sball(nc, g2, "qrow", [1, 512], F32)
        q2row = sball(nc, g2, "q2row", [1, 512], F32)

        ew(P, "gpsimd", "memset", Hn[:, :, :], 0.0)
        P.barrier()

        msk = C.msk_su if fwd else C.msk_sl
        rank_order = list(range(4)) if fwd else list(range(3, -1, -1))
        for r in rank_order:
            ccflat = cc_out.ap().rearrange("a b t -> (a b) t")
            for i in range(4):
                idma(P, gxs[:, i, :], ccflat, hidx[:, i:i + 1], eoff=(r * 4352) * 2048)
                idma(P, gzs[:, i, :], ccflat, hidx[:, 4 + i:5 + i],
                     eoff=(r * 4352 + 2048) * 2048)
            dma(P, gbc[:, :, :],
                cc_out.ap()[r, :, :].rearrange("(c p) t -> p c t", p=128)[:, 32:34, :])
            idma(P, gdt[:, :], ccd_out.ap().rearrange("a b t -> (a b) t"),
                 didx[:, 0:1], eoff=(r * 32) * 2048)
            P.barrier()
            if upto == "load0":
                for i in range(4):
                    dma_cast(P, fmv(xo, i, 0, 2048), gxs[:, i, :])
                    dma_cast(P, fmv(xo, 4 + i, 0, 2048), gzs[:, i, :])
                P.emit(nc, sems)
                return nc
            for i in range(4):
                act(P, gzsil[:, i, :], gzs[:, i, :], AF.Silu)
            P.barrier()
            sub_order = list(range(4)) if fwd else list(range(3, -1, -1))
            for j in sub_order:
                gst = 4 * r + j
                jsl = slice(j * 512, (j + 1) * 512)
                ew(P, "vector", "tensor_scalar_mul", lgA[:, :], gdt[:, jsl], a8[:, 0:1])
                P.barrier()
                for c in range(4):
                    sl = slice(c * 128, (c + 1) * 128)
                    if fwd:
                        ew(P, "vector", "tensor_tensor_scan", lgs[:, sl], onesr[0:8, 0:128],
                           lgA[:, sl], 0.0, ALU.mult, ALU.add)
                    else:
                        ew(P, "vector", "reduce_sum", srow[:, :], lgA[:, sl], AX.X)
                        ew(P, "vector", "tensor_tensor_scan", lgs[:, sl], onesr[0:8, 0:128],
                           lgA[:, sl], 0.0, ALU.mult, ALU.add)
                        P.barrier()
                        ew(P, "vector", "tensor_sub", egr2[:, sl], lgA[:, sl], lgs[:, sl])
                        P.barrier()
                        ew(P, "vector", "tensor_scalar_add", lgs[:, sl], egr2[:, sl], srow[:, 0:1])
                        P.barrier()
                P.barrier()
                act(P, eglr[:, :], lgs[:, :], AF.Exp)
                ew(P, "vector", "tensor_scalar_mul", nlg[:, :], lgs[:, :], -1.0)
                P.barrier()
                for c in range(4):
                    sl = slice(c * 128, (c + 1) * 128)
                    cT = c * 128 + 127 if fwd else c * 128
                    ew(P, "vector", "tensor_scalar", egtmp[:, sl], lgs[:, sl],
                       lgs[:, cT:cT + 1], -1.0, ALU.subtract, ALU.mult)
                P.barrier()
                act(P, egr2[:, :], egtmp[:, :], AF.Exp)
                P.barrier()
                if upto == "rows0":
                    dma(P, xo.rearrange("(c p) t -> c p t", p=128)[0, 0:8, 0:512], lgs[:, :])
                    dma(P, xo.rearrange("(c p) t -> c p t", p=128)[1, 0:8, 0:512], eglr[:, :])
                    dma(P, xo.rearrange("(c p) t -> c p t", p=128)[2, 0:8, 0:512], egr2[:, :])
                    dma(P, xo.rearrange("(c p) t -> c p t", p=128)[3, 0:8, 0:512], gdt[:, 0:512])
                    P.emit(nc, sems)
                    return nc
                for h in range(8):
                    bank = ps[4 + h % 2]
                    mm(P, bank[:, :], C.bsel8[:, h * 128:(h + 1) * 128], eglr[0:8, :])
                    P.barrier()
                    ew(P, "vector", "tensor_mul",
                       chat8[:, h, :], gbc[:, 1, jsl], bank[:, :])
                    P.barrier()
                for pc in range(4):
                    bank = ps[4 + pc % 2]
                    h0, h1 = 2 * pc, 2 * pc + 1
                    mm(P, bank[0:64, :], C.bsel8[:, h0 * 128:h0 * 128 + 64], gdt[0:8, jsl])
                    mm(P, bank[64:128, :], C.bsel8[:, h1 * 128:h1 * 128 + 64], gdt[0:8, jsl])
                    P.barrier()
                    ew(P, "vector", "tensor_mul",
                       xhat4[:, pc, :], gxs[:, pc, jsl], bank[:, :])
                    P.barrier()
                if upto == "prep0":
                    for h in range(4):
                        dma(P, xo.rearrange("(c p) t -> c p t", p=128)[h, :, 0:512], chat8[:, h, :])
                    dma_cast(P, xo.rearrange("(c p) t -> c p t", p=128)[4, :, 0:512], xhat4[:, 0, :])
                    P.emit(nc, sems)
                    return nc
                corder = list(range(4)) if fwd else list(range(3, -1, -1))
                for c in corder:
                    sl = slice(c * 128, (c + 1) * 128)
                    gsl = slice(j * 512 + c * 128, j * 512 + (c + 1) * 128)
                    cT = c * 128 + 127 if fwd else c * 128
                    skip = (upto or "").split("-")[1:] if upto else []
                    if "notr" not in skip:
                        tr(P, pT[:, 0:128], gbc[:, 0, gsl], C.identb)
                        for pc in range(4):
                            tr(P, pT[:, 128 * (pc + 1):128 * (pc + 2)], xhat4[:, pc, sl], C.identb)
                        tr(P, ps[6][:, 0:8], egr2[:, sl], C.identf[0:8, 0:8])
                    if "nogram" not in skip:
                        mm(P, ps[6][:, 128:256], gbc[:, 0, gsl], gbc[:, 1, gsl])
                    if "noD" not in skip:
                        for h in range(8):
                            bank = ps[4 + h // 4]
                            o = (h % 4) * 128
                            mm(P, bank[:, o:o + 128], C.bsel8[:, h * 128:(h + 1) * 128], lgs[0:8, sl],
                               start=True, stop=False)
                            mm(P, bank[:, o:o + 128], nlg[0:8, sl], C.bsel8[:, h * 128:(h + 1) * 128],
                               start=False, stop=False)
                            mm(P, bank[:, o:o + 128], C.identf, msk, start=False, stop=True)
                    P.barrier()
                    if "nocB" in skip:
                        P.barrier()
                        if upto and upto.startswith("cB0"):
                            dma(P, fmv(xo, 0, 0, 128), btok_f[:, :])
                            P.emit(nc, sems)
                            return nc
                    if "noev1" not in skip:
                        ew(P, "vector", "tensor_copy", btok_f[:, :], pT[:, 0:128])
                    if "noxh" not in skip:
                        for pc in range(4):
                            ew(P, "vector", "tensor_copy",
                               xhtok_c[:, pc, :], pT[:, 128 * (pc + 1):128 * (pc + 2)])
                    if "nomisc" not in skip:
                        ew(P, "vector", "tensor_copy", tcole[:, :], ps[6][:, 0:8])
                        ew(P, "vector", "tensor_copy", gsb[:, :], ps[6][:, 128:256])
                    if "noexp" not in skip:
                        act(P, dsb[:, 0, :], ps[4][:, :], AF.Exp)
                        act(P, dsb[:, 1, :], ps[5][:, :], AF.Exp)
                    P.barrier()
                    if upto and upto.startswith("cB0"):
                        dma(P, fmv(xo, 0, 0, 128), btok_f[:, :])
                        dma(P, fmv(xo, 1, 0, 128), gsb[:, :])
                        dma(P, xo.rearrange("(c p) t -> c p t", p=128)[2, :, 0:8], tcole[:, :])
                        ew(P, "vector", "tensor_copy", gbuf[:, 0:512], ps[4][:, :])
                        P.barrier()
                        dma(P, fmv(xo, 3, 0, 512), gbuf[:, 0:512])
                        P.emit(nc, sems)
                        return nc
                    for half in range(2):
                        ew(P, "vector", "tensor_tensor",
                           ghat_c[:, half * 4:(half + 1) * 4, :],
                           gsb[:, None, :].broadcast_to([128, 4, 128]),
                           dsb[:, half, :].rearrange("p (a b) -> p a b", a=4), ALU.mult)
                    for h in range(8):
                        ew(P, "vector", "tensor_scalar_mul",
                           xgtok_c[:, h, :], xhtok_c[:, h // 2, (h % 2) * 64:(h % 2) * 64 + 64],
                           tcole[:, h:h + 1])
                    P.barrier()
                    if upto == "cC0":
                        dma_cast(P, xo.rearrange("(c p) t -> c p t", p=128)[0, :, 0:1024],
                                 ghat_c.rearrange("p a b -> p (a b)")[:, 0:1024])
                        dma(P, xo.rearrange("(c p) t -> c p t", p=128)[2, :, 0:512],
                            xgtok_c.rearrange("p a b -> p (a b)")[:, 0:512])
                        P.emit(nc, sems)
                        return nc
                    for h in range(8):
                        pair = h // 2
                        rsl = slice((h % 2) * 64, (h % 2) * 64 + 64)
                        mm(P, ps[pair][rsl, sl], xhtok_c[:, pair, rsl], ghat_c[:, h, :],
                           start=True, stop=False)
                        mm(P, ps[pair][rsl, sl], Hn[:, h, :], chat8[:, h, sl],
                           start=False, stop=True)
                        mm(P, ps[6][:, 16 + h:17 + h], C.bsel8[:, h * 128:(h + 1) * 128],
                           eglr[0:8, cT:cT + 1])
                        mm(P, ps[4][:, h * 64:(h + 1) * 64], btok_f[:, :], xgtok_c[:, h, :],
                           start=True, stop=True)
                    P.barrier()
                    if upto == "cD0":
                        ew(P, "vector", "tensor_copy", gbuf[:, :], ps[0][:, :])
                        P.barrier()
                        dma(P, fmv(xo, 0, 0, 512), gbuf[:, :])
                        P.emit(nc, sems)
                        return nc
                    for h in range(8):
                        ew(P, "vector", "tensor_scalar_mul", hscl[:, h % 2, :], Hn[:, h, :],
                           ps[6][:, 16 + h:17 + h])
                        ew(P, "vector", "scalar_tensor_tensor", Hn[:, h, :],
                           ps[4][:, h * 64:(h + 1) * 64], 1.0, hscl[:, h % 2, :],
                           ALU.mult, ALU.add)
                    P.barrier()
                if upto == "chunk0":
                    for pair in range(4):
                        ew(P, "vector", "tensor_copy", gbuf[:, :], ps[pair][:, :])
                        P.barrier()
                        dma(P, fmv(xo, pair, 0, 512), gbuf[:, :])
                        P.barrier()
                    dma(P, xo.rearrange("(c p) t -> c p t", p=128)[4, :, 0:512],
                        Hn.rearrange("p a b -> p (a b)")[:, 0:512])
                    P.emit(nc, sems)
                    return nc
                for pair in range(4):
                    ew(P, "vector", "scalar_tensor_tensor", ybuf[:, :], gxs[:, pair, jsl],
                       dvt[:, pair:pair + 1], ps[pair][:, :], ALU.mult, ALU.add)
                    P.barrier()
                    ew(P, "vector", "tensor_mul", gbuf[:, :], ybuf[:, :], gzsil[:, pair, jsl])
                    P.barrier()
                    act(P, sqs[:, :], gbuf[:, :], AF.Square)
                    P.barrier()
                    mm(P, ps[5][0:1, 0:512], C.onescf[:, 0:1], sqs[:, :],
                       start=(pair == 0), stop=(pair == 3))
                    dma(P, gsp[pair, :, gst * 512:(gst + 1) * 512], gbuf[:, :])
                    P.barrier()
                ew(P, "vector", "tensor_copy", qrow[0:1, :], ps[5][0:1, 0:512])
                P.barrier()
                dma(P, ccq_in.ap()[0:1, gst * 512:(gst + 1) * 512], qrow[0:1, :])
                P.barrier()

        if upto == "scan":
            for pair in range(4):
                dma(P, gbuf[:, :], gsp[pair, :, 0:512])
                P.barrier()
                dma(P, fmv(xo, pair, 0, 512), gbuf[:, :])
                P.barrier()
            P.emit(nc, sems)
            return nc

        coll(P, "AllReduce", ALU.add, [ccq_in.ap().opt()], [ccq_out.ap().opt()])
        P.barrier()
        for gst in range(16):
            t0 = gst * 512
            for pair in range(4):
                dma(P, gbuf[:, :] if pair == 0 else sqs[:, :] if pair == 1 else
                    ybuf[:, :] if pair == 2 else chat8[:, 0, :],
                    gsp[pair, :, t0:t0 + 512])
            dma(P, qrow[0:1, :], ccq_out.ap()[0:1, t0:t0 + 512])
            P.barrier()
            ew(P, "vector", "tensor_scalar_mul", qrow[0:1, :], qrow[0:1, :], 1.0 / 2048)
            P.barrier()
            act(P, q2row[0:1, :], qrow[0:1, :], AF.Ln, bias=1e-5)
            P.barrier()
            act(P, qrow[0:1, :], q2row[0:1, :], AF.Exp, scale=-0.5)
            P.barrier()
            mm(P, ps[6][:, 0:512], C.ones1f[0:1, 0:128], qrow[0:1, :])
            P.barrier()
            bufs = [gbuf, sqs, ybuf, chat8[:, 0, :]]
            for pair in range(4):
                sb = bufs[pair]
                ew(P, "vector", "tensor_mul", xhat4[:, pair, :], sb, ps[6][:, 0:512])
            P.barrier()
            for oc in range(8):
                bank = ps[oc % 2]
                for jt in range(4):
                    mm(P, bank[:, :], wfm[:, jt, oc * 128:(oc + 1) * 128],
                       xhat4[:, jt, :], start=(jt == 0), stop=(jt == 3))
                P.barrier()
                ew(P, "vector", "tensor_copy", gbuf[:, :], bank[:, :])
                P.barrier()
                dma(P, ccr_in.ap().rearrange("(q c p) t -> q c p t", q=4, p=128)
                    [gst // 4, oc, :, (gst % 4) * 512:(gst % 4 + 1) * 512], gbuf[:, :])
                P.barrier()

    coll(P, "ReduceScatter", ALU.add, [ccr_in.ap().opt()], [ccr_out.ap().opt()])
    P.barrier()

    with ExitStack() as gf:
        xrc = sball(nc, gf, "xrc", [128, 512], F32)
        occ = sball(nc, gf, "occ", [128, 512], F32)
        ob = sball(nc, gf, "ob", [128, 512], F32)
        xnc = sball(nc, gf, "xnc", [128, 512], BF16)
        for oc in range(8):
            for tc in range(4):
                t0 = tc * 512
                dma(P, xrc[:, :], fmv(xresd, oc, t0, t0 + 512))
                dma(P, occ[:, :], fmv(ccr_out.ap(), oc, t0, t0 + 512))
                if fwd:
                    dma(P, xnc[:, :], fmv(xnsp, oc, t0, t0 + 512))
                P.barrier()
                ew(P, "vector", "scalar_tensor_tensor", ob[:, :], occ[:, :],
                   pbt[:, oc:oc + 1], xrc[:, :], ALU.add, ALU.add)
                P.barrier()
                if fwd:
                    ew(P, "vector", "tensor_add", ob[:, :], ob[:, :], xnc[:, :])
                    P.barrier()
                dma(P, fmv(xo, oc, t0, t0 + 512), ob[:, :])
                P.barrier()

    P.emit(nc, sems)
    return nc



def build_l0ffn():
    nc = mknc()
    xin = nc.dram_tensor("xin", [1024, 2048], F32, kind="ExternalInput").ap()
    wgd = nc.dram_tensor("wg_r", [128, 8, 8], F32, kind="ExternalInput").ap()
    wxd = nc.dram_tensor("wexp", [1024, 1024], F32, kind="ExternalInput").ap()
    bxd = nc.dram_tensor("bexp_r", [128, 8], F32, kind="ExternalInput").ap()
    mpd = nc.dram_tensor("maskprev", [8, 1], F32, kind="ExternalInput").ap()
    xo = nc.dram_tensor("xo", [1024, 2048], F32, kind="ExternalOutput").ap()

    cnt_in = nc.dram_tensor("cnt_in", [1, 8], F32)
    cnt_out = nc.dram_tensor("cnt_out", [8, 8], F32)

    P = Prog()
    ps = alloc_psums(nc)
    C = Consts(nc, P)
    wgt = nc.alloc_sbuf_tensor("wgt", [128, 8, 8], BF16).ap()
    bxt = nc.alloc_sbuf_tensor("bxt", [128, 8], F32).ap()
    mpt = nc.alloc_sbuf_tensor("mpt", [8, 1], F32).ap()
    dma_cast(P, wgt[:, :, :], wgd[:, :, :])
    dma(P, bxt[:, :], bxd[:, :])
    dma(P, mpt[:, :], mpd[:, :])
    P.barrier()

    with ExitStack() as g1:
        xn_bf = sball(nc, g1, "xn_bf", [128, 8, 2048], BF16)
        xt = sball(nc, g1, "xt", [128, 8, 512], F32)
        scr = sball(nc, g1, "scr", [128, 512], F32)
        scr2 = sball(nc, g1, "scr2", [128, 512], F32)
        murow = sball(nc, g1, "murow", [1, 512], F32)
        rstdrow = sball(nc, g1, "rstdrow", [1, 512], F32)
        lrows = sball(nc, g1, "lrows", [8, 2048], F32)
        ltok = sball(nc, g1, "ltok", [128, 16, 8], F32)
        ohb = sball(nc, g1, "ohb", [128, 16, 8], BF16)
        ctok = sball(nc, g1, "ctok", [128, 16, 8], F32)
        gcols = sball(nc, g1, "gcols", [128, 16], F32)
        mrow = sball(nc, g1, "mrow", [128, 1], F32)
        erow = sball(nc, g1, "erow", [128, 8], F32)
        srow = sball(nc, g1, "srow", [128, 1], F32)
        carry = sball(nc, g1, "carry", [1, 8], F32)
        obc = sball(nc, g1, "obc", [128, 8], F32)
        gT = sball(nc, g1, "gT", [4, 128], F32)
        gbs = sball(nc, g1, "gbs", [128, 512], F32)
        wx_bf = sball(nc, g1, "wx_bf", [128, 8, 1024], BF16)
        xrc = sball(nc, g1, "xrc", [128, 512], F32)
        ob = sball(nc, g1, "ob", [128, 512], F32)

        subs = [(0, 512), (512, 512), (1024, 512), (1536, 512)]
        emit_ln(P, C, ps,
                lambda kc, t0, w, dst: dma(P, dst, fmv(xin, kc, t0, t0 + w)),
                lambda kc, t0, w: xn_bf[:, kc, t0:t0 + w],
                subs, xt, scr, scr2, murow, rstdrow)

        for tc in range(4):
            for kc in range(8):
                mm(P, ps[0][0:8, 0:512], wgt[:, kc, :], xn_bf[:, kc, tc * 512:(tc + 1) * 512],
                   start=(kc == 0), stop=(kc == 7))
            P.barrier()
            ew(P, "vector", "tensor_copy", lrows[:, tc * 512:(tc + 1) * 512], ps[0][0:8, 0:512])
            P.barrier()
        for c in range(16):
            tr(P, ps[0][:, 0:8], lrows[:, c * 128:(c + 1) * 128], C.identf[0:8, 0:8])
            P.barrier()
            ew(P, "vector", "tensor_copy", ltok[:, c, :], ps[0][:, 0:8])
            P.barrier()
        ew(P, "gpsimd", "memset", carry[:, :], 0.0)
        P.barrier()
        for c in range(16):
            ew(P, "vector", "reduce_max", mrow[:, :], ltok[:, c, :], AX.X)
            P.barrier()
            ew(P, "vector", "tensor_scalar_mul", srow[:, :], mrow[:, :], -1.0)
            P.barrier()
            act(P, erow[:, :], ltok[:, c, :], AF.Exp, bias=srow[:, 0:1])
            P.barrier()
            ew(P, "vector", "reduce_sum", srow[:, :], erow[:, :], AX.X)
            P.barrier()
            ew(P, "vector", "reciprocal", gcols[:, c:c + 1], srow[:, :])
            ew(P, "vector", "tensor_scalar", ohb[:, c, :], ltok[:, c, :], mrow[:, 0:1],
               None, ALU.is_ge, ALU.bypass)
            P.barrier()
            mm(P, ps[1][:, 0:8], C.triu_onesb[:, :], ohb[:, c, :], start=True, stop=False)
            mm(P, ps[1][:, 0:8], C.ones1f[0:1, 0:128], carry[0:1, :], start=False, stop=True)
            P.barrier()
            ew(P, "vector", "tensor_copy", ctok[:, c, :], ps[1][:, 0:8])
            P.barrier()
            mm(P, ps[2][0:1, 0:8], C.sel127[:, 0:1], ctok[:, c, :])
            P.barrier()
            ew(P, "vector", "tensor_copy", carry[0:1, :], ps[2][0:1, 0:8])
            P.barrier()
        dma(P, cnt_in.ap()[0:1, :], carry[0:1, :])
        P.barrier()
        coll(P, "AllGather", ALU.bypass, [cnt_in.ap().opt()], [cnt_out.ap().opt()],
             rg=[[0, 1, 2, 3, 4, 5, 6, 7]])
        P.barrier()
        with ExitStack() as g2:
            gat = sball(nc, g2, "gat", [8, 8], F32)
            dma(P, gat[:, :], cnt_out.ap()[:, :])
            P.barrier()
            mm(P, ps[2][0:1, 0:8], mpt[:, 0:1], gat[:, :])
            P.barrier()
            mm(P, ps[3][:, 0:8], C.ones1f[0:1, 0:128], ps[2][0:1, 0:8]) if False else None
            ew(P, "vector", "tensor_copy", carry[0:1, :], ps[2][0:1, 0:8])
            P.barrier()
        mm(P, ps[3][:, 0:8], C.ones1f[0:1, 0:128], carry[0:1, :])
        P.barrier()
        ew(P, "vector", "tensor_copy", obc[:, :], ps[3][:, 0:8])
        P.barrier()
        for c in range(16):
            ew(P, "vector", "tensor_add", erow[:, :], ctok[:, c, :], obc[:, :])
            P.barrier()
            ew(P, "vector", "tensor_single_scalar", erow[:, :], erow[:, :], 100.5, ALU.is_le)
            P.barrier()
            ew(P, "vector", "tensor_mul", erow[:, :], erow[:, :], ohb[:, c, :])
            P.barrier()
            ew(P, "vector", "reduce_sum", srow[:, :], erow[:, :], AX.X)
            P.barrier()
            ew(P, "vector", "tensor_mul", gcols[:, c:c + 1], gcols[:, c:c + 1], srow[:, :])
            P.barrier()
        for kc in range(8):
            dma_cast(P, wx_bf[:, kc, :],
                     wxd.rearrange("(c p) o -> c p o", p=128)[kc, :, :])
        P.barrier()
        for tc in range(4):
            tr(P, ps[2][0:4, 0:128], gcols[:, 4 * tc:4 * tc + 4], C.identf)
            P.barrier()
            ew(P, "vector", "tensor_copy", gT[:, :], ps[2][0:4, 0:128])
            P.barrier()
            for cc in range(4):
                mm(P, ps[3][:, cc * 128:(cc + 1) * 128],
                   C.bsel8[0:4, cc * 128:cc * 128 + 128], gT[0:4, :])
            P.barrier()
            ew(P, "vector", "tensor_copy", gbs[:, :], ps[3][:, :])
            P.barrier()
            for oc in range(8):
                bank = ps[oc % 2]
                for kc in range(8):
                    mm(P, bank[:, :], wx_bf[:, kc, oc * 128:(oc + 1) * 128],
                       xn_bf[:, kc, tc * 512:(tc + 1) * 512], start=(kc == 0), stop=(kc == 7))
                P.barrier()
                dma(P, xrc[:, :], fmv(xin, oc, tc * 512, (tc + 1) * 512))
                P.barrier()
                ew(P, "vector", "scalar_tensor_tensor", scr[:, :], bank[:, :],
                   bxt[:, oc:oc + 1], gbs[:, :], ALU.add, ALU.mult)
                P.barrier()
                ew(P, "vector", "tensor_add", ob[:, :], scr[:, :], xrc[:, :])
                P.barrier()
                dma(P, fmv(xo, oc, tc * 512, (tc + 1) * 512), ob[:, :])
                P.barrier()

    P.emit(nc, None)
    return nc


def stage_l0ffn(d, xcur):
    wg = d["L0_ffn_w_gate"].astype(np.float32)
    wg_r = np.ascontiguousarray(wg.reshape(8, 128, 8).transpose(1, 0, 2))
    wexp = np.ascontiguousarray(d["L0_ffn_w_expert"].astype(np.float32))
    bexp = np.ascontiguousarray(
        d["L0_ffn_b_expert"].astype(np.float32).reshape(8, 128).T)
    maps = []
    for core in range(8):
        g, qc = core // 4, core % 4
        t0 = qc * 2048
        xin = np.ascontiguousarray(xcur[g, t0:t0 + 2048].T).astype(np.float32)
        mp = np.zeros((8, 1), np.float32)
        mp[:core, 0] = 1.0
        m = dict(xin=xin, wg_r=wg_r, wexp=wexp, bexp_r=bexp, maskprev=mp)
        m.update(const_inputs())
        maps.append(m)
    return maps



def build_l1attna():
    nc = mknc()
    xin = nc.dram_tensor("xin", [1024, 2048], F32, kind="ExternalInput").ap()
    wqTd = nc.dram_tensor("wqT", [1024, 1024], F32, kind="ExternalInput").ap()
    wvTd = nc.dram_tensor("wvT", [1024, 1024], F32, kind="ExternalInput").ap()
    wkd = nc.dram_tensor("wk", [1024, 1024], F32, kind="ExternalInput").ap()
    wbd = nc.dram_tensor("wbeta", [1024, 16], F32, kind="ExternalInput").ap()
    bbd = nc.dram_tensor("bbeta", [16, 1], F32, kind="ExternalInput").ap()
    wod = nc.dram_tensor("wo", [1024, 1024], F32, kind="ExternalInput").ap()
    bod = nc.dram_tensor("bo_r", [128, 8], F32, kind="ExternalInput").ap()
    lngd = nc.dram_tensor("lng_r", [128, 8], F32, kind="ExternalInput").ap()
    lnbd = nc.dram_tensor("lnb_r", [128, 8], F32, kind="ExternalInput").ap()
    m2d = nc.dram_tensor("m2sb", [4, 4], F32, kind="ExternalInput").ap()
    b4d = nc.dram_tensor("b4", [1, 4], F32, kind="ExternalInput").ap()
    xo = nc.dram_tensor("xo", [1024, 2048], F32, kind="ExternalOutput").ap()

    ccs_in = nc.dram_tensor("ccs_in", [128, 9], F32)
    ccs_out = nc.dram_tensor("ccs_out", [4, 128, 9], F32)
    osp = nc.dram_tensor("osp", [1024, 2048], BF16).ap()

    P = Prog()
    ps = alloc_psums(nc)
    C = Consts(nc, P)
    bbt = nc.alloc_sbuf_tensor("bbt", [16, 1], F32).ap()
    bot = nc.alloc_sbuf_tensor("bot", [128, 8], F32).ap()
    lng = nc.alloc_sbuf_tensor("lng", [128, 8], F32).ap()
    lnb = nc.alloc_sbuf_tensor("lnb", [128, 8], F32).ap()
    m2t = nc.alloc_sbuf_tensor("m2t", [4, 4], F32).ap()
    b4t = nc.alloc_sbuf_tensor("b4t", [1, 4], F32).ap()
    wqs = nc.alloc_sbuf_tensor("wqs", [128, 8, 16], BF16).ap()
    wvs = nc.alloc_sbuf_tensor("wvs", [128, 8, 16], BF16).ap()
    wbt = nc.alloc_sbuf_tensor("wbt", [128, 8, 16], BF16).ap()
    dma(P, bbt[:, :], bbd[:, :])
    dma(P, bot[:, :], bod[:, :])
    dma(P, lng[:, :], lngd[:, :])
    dma(P, lnb[:, :], lnbd[:, :])
    dma(P, m2t[:, :], m2d[:, :])
    dma(P, b4t[:, :], b4d[:, :])
    P.barrier()

    with ExitStack() as g0:
        wT = sball(nc, g0, "wT", [128, 8, 1024], BF16)
        for dram, dst in ((wqTd, wqs), (wvTd, wvs)):
            for jc in range(8):
                dma_cast(P, wT[:, jc, :], dram.rearrange("(c p) o -> c p o", p=128)[jc, :, :])
            P.barrier()
            for dc in range(8):
                for jc in range(8):
                    mm(P, ps[0][:, 2 * jc:2 * jc + 2], wT[:, jc, dc * 128:(dc + 1) * 128],
                       C.hs2b[:, :], start=True, stop=True)
                P.barrier()
                ew(P, "vector", "tensor_copy", dst[:, dc, :], ps[0][:, 0:16])
                P.barrier()
        for dc in range(8):
            dma_cast(P, wbt[:, dc, :], wbd.rearrange("(c p) o -> c p o", p=128)[dc, :, :])
        P.barrier()

    with ExitStack() as g1:
        xn_bf = sball(nc, g1, "xn_bf", [128, 8, 2048], BF16)
        w_bf = sball(nc, g1, "w_bf", [128, 8, 2048], BF16)
        bbf = sball(nc, g1, "bbf", [128, 512], F32)
        obuf = sball(nc, g1, "obuf", [128, 8, 512], BF16)
        xt = sball(nc, g1, "xt", [128, 8, 512], F32)
        scr = sball(nc, g1, "scr", [128, 512], F32)
        scr2 = sball(nc, g1, "scr2", [128, 512], F32)
        murow = sball(nc, g1, "murow", [1, 512], F32)
        rstdrow = sball(nc, g1, "rstdrow", [1, 512], F32)
        qsr = sball(nc, g1, "qsr", [16, 2048], F32)
        vsr = sball(nc, g1, "vsr", [16, 2048], F32)
        bet = sball(nc, g1, "bet", [16, 2048], F32)
        lnbtc = sball(nc, g1, "lnbtc", [16, 512], F32)
        lnb4 = sball(nc, g1, "lnb4", [16, 4], F32)
        lsum = sball(nc, g1, "lsum", [16, 1], F32)
        ktmp = sball(nc, g1, "ktmp", [128, 512], BF16)
        cloc = sball(nc, g1, "cloc", [128, 4, 512], F32)
        cend = sball(nc, g1, "cend", [128, 8], F32)
        cstart = sball(nc, g1, "cstart", [128, 8], F32)
        gce = sball(nc, g1, "gce", [128, 4, 9], F32)
        lnT = sball(nc, g1, "lnT", [4, 16], F32)
        wmat = sball(nc, g1, "wmat", [16, 4], F32)
        tmpc = sball(nc, g1, "tmpc", [128, 4], F32)
        ytc = sball(nc, g1, "ytc", [128, 8, 512], F32)
        ob = sball(nc, g1, "ob", [128, 512], F32)

        subs = [(0, 512), (512, 512), (1024, 512), (1536, 512)]
        emit_ln(P, C, ps,
                lambda kc, t0, w, dst: dma(P, dst, fmv(xin, kc, t0, t0 + w)),
                lambda kc, t0, w: xn_bf[:, kc, t0:t0 + w],
                subs, xt, scr, scr2, murow, rstdrow)

        for tc in range(4):
            tsl = slice(tc * 512, (tc + 1) * 512)
            for dst, wsum in ((qsr, wqs), (vsr, wvs)):
                for dc in range(8):
                    mm(P, ps[0][0:16, 0:512], wsum[:, dc, :], xn_bf[:, dc, tsl],
                       start=(dc == 0), stop=(dc == 7))
                P.barrier()
                ew(P, "vector", "tensor_copy", dst[:, tsl], ps[0][0:16, 0:512])
                P.barrier()
            for dc in range(8):
                mm(P, ps[0][0:16, 0:512], wbt[:, dc, :], xn_bf[:, dc, tsl],
                   start=(dc == 0), stop=(dc == 7))
            P.barrier()
            act(P, bet[:, tsl], ps[0][0:16, 0:512], AF.Sigmoid, bias=bbt[:, 0:1])
            P.barrier()

        with ExitStack() as g2:
            wkb = sball(nc, g2, "wkb", [128, 8, 128], BF16)
            for oc in range(8):
                h0, h1 = 2 * oc, 2 * oc + 1
                for dc in range(8):
                    dma_cast(P, wkb[:, dc, :],
                             wkd.rearrange("(c p) o -> c p o", p=128)[dc, :, oc * 128:(oc + 1) * 128])
                P.barrier()
                for tc in range(4):
                    tsl = slice(tc * 512, (tc + 1) * 512)
                    bank = ps[tc % 2]
                    for dc in range(8):
                        mm(P, bank[:, :], wkb[:, dc, :], xn_bf[:, dc, tsl],
                           start=(dc == 0), stop=(dc == 7))
                    P.barrier()
                    ew(P, "vector", "tensor_copy", ktmp[:, :], bank[:, :])
                    mm(P, ps[4][0:64, 0:512], C.bsel16[:, h0 * 128:h0 * 128 + 64], vsr[0:16, tsl])
                    mm(P, ps[4][64:128, 0:512], C.bsel16[:, h1 * 128:h1 * 128 + 64], vsr[0:16, tsl])
                    P.barrier()
                    ew(P, "vector", "tensor_mul", w_bf[:, oc, tsl], ktmp[:, :], ps[4][:, 0:512])
                    P.barrier()

        for tc in range(4):
            act(P, lnbtc[:, :], bet[:, tc * 512:(tc + 1) * 512], AF.Ln)
            P.barrier()
            ew(P, "vector", "reduce_sum", lnb4[:, tc:tc + 1], lnbtc[:, :], AX.X)
            P.barrier()
        ew(P, "vector", "reduce_sum", lsum[:, :], lnb4[:, :], AX.X)
        P.barrier()
        for kc in range(8):
            h0, h1 = 2 * kc, 2 * kc + 1
            for tc in range(4):
                tsl = slice(tc * 512, (tc + 1) * 512)
                mm(P, ps[5][0:64, 0:512], C.bsel16[:, h0 * 128:h0 * 128 + 64], bet[0:16, tsl])
                mm(P, ps[5][64:128, 0:512], C.bsel16[:, h1 * 128:h1 * 128 + 64], bet[0:16, tsl])
                P.barrier()
                ew(P, "vector", "tensor_copy", bbf[:, :], ps[5][:, 0:512])
                P.barrier()
                ini = 0.0 if tc == 0 else cloc[:, tc - 1, 511:512]
                ew(P, "vector", "tensor_tensor_scan", cloc[:, tc, :], bbf[:, :],
                   w_bf[:, kc, tsl], ini, ALU.mult, ALU.add)
                P.barrier()
            ew(P, "vector", "tensor_copy", cend[:, kc:kc + 1], cloc[:, 3, 511:512])
            P.barrier()
        dma(P, ccs_in.ap()[:, 0:8], cend[:, :])
        dma_nc(P, ccs_in.ap()[0:16, 8:9], lsum[:, :])
        P.barrier()
        coll(P, "AllGather", ALU.bypass, [ccs_in.ap().opt()], [ccs_out.ap().opt()])
        P.barrier()
        dma(P, gce[:, :, :], ccs_out.ap().rearrange("r p c -> p r c"))
        P.barrier()
        tr(P, ps[0][0:4, 0:16], gce[0:16, :, 8], C.identf[0:16, 0:16])
        P.barrier()
        ew(P, "vector", "tensor_copy", lnT[:, :], ps[0][0:4, 0:16])
        P.barrier()
        mm(P, ps[0][0:16, 0:4], lnT[0:4, :], m2t[:, :], start=True, stop=False)
        mm(P, ps[0][0:16, 0:4], C.ones1f[0:1, 0:16], b4t[0:1, :], start=False, stop=True)
        P.barrier()
        act(P, wmat[:, :], ps[0][0:16, 0:4], AF.Exp)
        P.barrier()
        for kc in range(8):
            mm(P, ps[1][:, 0:4], C.bsel16[:, kc * 128:(kc + 1) * 128], wmat[0:16, :])
            P.barrier()
            ew(P, "vector", "tensor_mul", tmpc[:, :], gce[:, :, kc], ps[1][:, 0:4])
            P.barrier()
            ew(P, "vector", "reduce_sum", cstart[:, kc:kc + 1], tmpc[:, :], AX.X)
            P.barrier()

        for kc in range(8):
            h0, h1 = 2 * kc, 2 * kc + 1
            for tc in range(4):
                tsl = slice(tc * 512, (tc + 1) * 512)
                mm(P, ps[5][0:64, 0:512], C.bsel16[:, h0 * 128:h0 * 128 + 64], bet[0:16, tsl])
                mm(P, ps[5][64:128, 0:512], C.bsel16[:, h1 * 128:h1 * 128 + 64], bet[0:16, tsl])
                P.barrier()
                ew(P, "vector", "tensor_copy", bbf[:, :], ps[5][:, 0:512])
                P.barrier()
                ini = cstart[:, kc:kc + 1] if tc == 0 else cloc[:, tc - 1, 511:512]
                ew(P, "vector", "tensor_tensor_scan", cloc[:, tc, :], bbf[:, :],
                   w_bf[:, kc, tsl], ini, ALU.mult, ALU.add)
                P.barrier()
                mm(P, ps[4][0:64, 0:512], C.bsel16[:, h0 * 128:h0 * 128 + 64], qsr[0:16, tsl])
                mm(P, ps[4][64:128, 0:512], C.bsel16[:, h1 * 128:h1 * 128 + 64], qsr[0:16, tsl])
                P.barrier()
                ew(P, "vector", "tensor_mul", ktmp[:, :], cloc[:, tc, :], ps[4][:, 0:512])
                P.barrier()
                dma(P, fmv(osp, kc, tc * 512, (tc + 1) * 512), ktmp[:, :])
                P.barrier()

        with ExitStack() as g3:
            wob = sball(nc, g3, "wob", [128, 8, 128], BF16)
            for tc in range(4):
                tsl = slice(tc * 512, (tc + 1) * 512)
                for dc in range(8):
                    dma(P, obuf[:, dc, :], fmv(osp, dc, tc * 512, (tc + 1) * 512))
                P.barrier()
                for oc in range(8):
                    for dc in range(8):
                        dma_cast(P, wob[:, dc, :],
                                 wod.rearrange("(c p) o -> c p o", p=128)[dc, :, oc * 128:(oc + 1) * 128])
                    P.barrier()
                    bank = ps[oc % 2]
                    for dc in range(8):
                        mm(P, bank[:, :], wob[:, dc, :], obuf[:, dc, :],
                           start=(dc == 0), stop=(dc == 7))
                    P.barrier()
                    ew(P, "vector", "scalar_tensor_tensor", ytc[:, oc, :], bank[:, :],
                       bot[:, oc:oc + 1], xn_bf[:, oc, tsl], ALU.add, ALU.add)
                    P.barrier()
                for kc in range(8):
                    mm(P, ps[2][0:1, 0:512], C.onescf[:, 0:1], ytc[:, kc, :],
                       start=(kc == 0), stop=(kc == 7))
                for kc in range(8):
                    s = scr if kc % 2 == 0 else scr2
                    act(P, s[:, :], ytc[:, kc, :], AF.Square)
                    P.barrier()
                    mm(P, ps[3][0:1, 0:512], C.onescf[:, 0:1], s[:, :],
                       start=(kc == 0), stop=(kc == 7))
                P.barrier()
                ew(P, "vector", "tensor_scalar_mul", murow[0:1, :], ps[2][0:1, 0:512], 1.0 / 1024)
                ew(P, "vector", "tensor_scalar_mul", rstdrow[0:1, :], ps[3][0:1, 0:512], 1.0 / 1024)
                P.barrier()
                ew(P, "vector", "tensor_mul", scr[0:1, 0:512], murow[0:1, :], murow[0:1, :])
                P.barrier()
                ew(P, "vector", "tensor_sub", rstdrow[0:1, :], rstdrow[0:1, :], scr[0:1, 0:512])
                P.barrier()
                act(P, scr2[0:1, 0:512], rstdrow[0:1, :], AF.Ln, bias=1e-5)
                P.barrier()
                act(P, rstdrow[0:1, :], scr2[0:1, 0:512], AF.Exp, scale=-0.5)
                P.barrier()
                mm(P, ps[2][:, 0:512], C.ones1f[0:1, 0:128], murow[0:1, :])
                mm(P, ps[3][:, 0:512], C.ones1f[0:1, 0:128], rstdrow[0:1, :])
                P.barrier()
                for oc in range(8):
                    dma(P, xt[:, oc, :], fmv(xin, oc, tc * 512, (tc + 1) * 512))
                P.barrier()
                for oc in range(8):
                    ew(P, "vector", "tensor_sub", scr[:, :], ytc[:, oc, :], ps[2][:, 0:512])
                    P.barrier()
                    ew(P, "vector", "tensor_mul", scr[:, :], scr[:, :], ps[3][:, 0:512])
                    P.barrier()
                    ew(P, "vector", "scalar_tensor_tensor", ob[:, :], scr[:, :],
                       lng[:, oc:oc + 1], xt[:, oc, :], ALU.mult, ALU.add)
                    P.barrier()
                    ew(P, "vector", "tensor_scalar_add", ob[:, :], ob[:, :], lnb[:, oc:oc + 1])
                    P.barrier()
                    dma(P, fmv(xo, oc, tc * 512, (tc + 1) * 512), ob[:, :])
                    P.barrier()

    P.emit(nc, None)
    return nc


def stage_l1attna(d, xcur):
    wqT = np.ascontiguousarray(d["L1_attn_a_Wq"].astype(np.float32).T)
    wvT = np.ascontiguousarray(d["L1_attn_a_Wv"].astype(np.float32).T)
    wk = np.ascontiguousarray(d["L1_attn_a_Wk"].astype(np.float32))
    wbeta = np.ascontiguousarray(d["L1_attn_a_Wbeta"].astype(np.float32))
    bb = d["L1_attn_a_b_beta"].astype(np.float32).reshape(16, 1)
    wo = np.ascontiguousarray(d["L1_attn_a_Wo"].astype(np.float32))
    bo = np.ascontiguousarray(d["L1_attn_a_b_o"].astype(np.float32).reshape(8, 128).T)
    lng = np.ascontiguousarray(d["L1_attn_a_ln_g"].astype(np.float32).reshape(8, 128).T)
    lnb = np.ascontiguousarray(d["L1_attn_a_ln_b"].astype(np.float32).reshape(8, 128).T)
    maps = []
    for core in range(8):
        g, qc = core // 4, core % 4
        xin = np.ascontiguousarray(xcur[g, qc * 2048:(qc + 1) * 2048].T).astype(np.float32)
        m2 = np.zeros((4, 4), np.float32)
        for i in range(4):
            for j in range(4):
                if j < i < qc or (j < i and i < qc):
                    m2[i, j] = 1.0
        m2 = np.zeros((4, 4), np.float32)
        for i in range(4):
            for j in range(4):
                if j < i and i < qc:
                    m2[i, j] = 1.0
        b4 = np.where(np.arange(4) < qc, 0.0, NEG).astype(np.float32).reshape(1, 4)
        m = dict(xin=xin, wqT=wqT, wvT=wvT, wk=wk, wbeta=wbeta, bbeta=bb, wo=wo,
                 bo_r=bo, lng_r=lng, lnb_r=lnb, m2sb=m2, b4=b4)
        m.update(const_inputs())
        maps.append(m)
    return maps



def relpos_indicator():
    NB2 = 1407
    deltas = 639 - np.arange(NB2)
    num_buckets = 16
    rb = (deltas > 0).astype(np.int64) * num_buckets
    arp = np.abs(deltas)
    max_exact = 8
    is_small = arp < max_exact
    with np.errstate(divide="ignore"):
        rl = np.log(np.maximum(arp, 1).astype(np.float64) / max_exact) / np.log(128 / 8)
    rl = (max_exact + (rl * (num_buckets - max_exact)).astype(np.int64))
    rl = np.minimum(rl, num_buckets - 1)
    bucket = rb + np.where(is_small, arp, rl)
    ind = np.zeros((32, NB2), np.float32)
    ind[bucket, np.arange(NB2)] = 1.0
    return ind


def build_l1attnb():
    from concourse.ap import AP as RawAP
    nc = mknc()
    xin = nc.dram_tensor("xin", [1024, 2048], F32, kind="ExternalInput").ap()
    wqkvd = nc.dram_tensor("wqkv", [1024, 3072], F32, kind="ExternalInput").ap()
    wod = nc.dram_tensor("wo", [1024, 1024], F32, kind="ExternalInput").ap()
    reld = nc.dram_tensor("rel", [32, 16], F32, kind="ExternalInput").ap()
    ind2d = nc.dram_tensor("ind2", [32, 1407], F32, kind="ExternalInput").ap()
    fLd = nc.dram_tensor("fL", [1, 256], F32, kind="ExternalInput").ap()
    fRd = nc.dram_tensor("fR", [1, 256], F32, kind="ExternalInput").ap()
    fXd = nc.dram_tensor("fX", [1, 256], F32, kind="ExternalInput").ap()
    emkd = nc.dram_tensor("emask2", [128, 28], F32, kind="ExternalInput").ap()
    bixd = nc.dram_tensor("bandidx", [128, 19], I32, kind="ExternalInput").ap()
    xo = nc.dram_tensor("xo", [1024, 2048], F32, kind="ExternalOutput").ap()

    cck_in = nc.dram_tensor("cck_in", [1024, 2048], BF16)
    cck_out = nc.dram_tensor("cck_out", [4, 1024, 2048], BF16)
    cckt_in = nc.dram_tensor("cckt_in", [2048, 1024], BF16)
    cckt_out = nc.dram_tensor("cckt_out", [4, 2048, 1024], BF16)
    ccv_in = nc.dram_tensor("ccv_in", [2048, 1024], BF16)
    ccv_out = nc.dram_tensor("ccv_out", [4, 2048, 1024], BF16)
    rband_h = nc.dram_tensor("rband", [16, 1407], F32)
    rband = rband_h.ap()

    P = Prog()
    ps = alloc_psums(nc)
    pT = ps[7]
    C = Consts(nc, P)
    emk = nc.alloc_sbuf_tensor("emk", [128, 28], F32).ap()
    bix = nc.alloc_sbuf_tensor("bix", [128, 19], I32).ap()
    fLt = nc.alloc_sbuf_tensor("fLt", [1, 256], F32).ap()
    fRt = nc.alloc_sbuf_tensor("fRt", [1, 256], F32).ap()
    fXt = nc.alloc_sbuf_tensor("fXt", [1, 256], F32).ap()
    relt = nc.alloc_sbuf_tensor("relt", [32, 16], BF16).ap()
    ebr = nc.alloc_sbuf_tensor("ebr", [16, 2], F32).ap()
    q_bf = nc.alloc_sbuf_tensor("q_bf", [128, 8, 2048], BF16).ap()
    o_bf = nc.alloc_sbuf_tensor("o_bf", [128, 8, 2048], BF16).ap()
    dma(P, emk[:, :], emkd[:, :])
    dma(P, bix[:, :], bixd[:, :])
    dma(P, fLt[:, :], fLd[:, :])
    dma(P, fRt[:, :], fRd[:, :])
    dma(P, fXt[:, :], fXd[:, :])
    dma_cast(P, relt[:, :], reld[:, :])
    P.barrier()

    with ExitStack() as g0:
        indb = sball(nc, g0, "indb", [32, 1407], BF16)
        brows = sball(nc, g0, "brows", [16, 1407], F32)
        dma_cast(P, indb[:, :], ind2d[:, :])
        P.barrier()
        for j in range(3):
            w = min(512, 1407 - j * 512)
            mm(P, ps[0][0:16, 0:w], relt[:, :], indb[:, j * 512:j * 512 + w])
            P.barrier()
            ew(P, "vector", "tensor_copy", brows[:, j * 512:j * 512 + w], ps[0][0:16, 0:w])
            P.barrier()
        dma(P, rband[:, :], brows[:, :])
        ew(P, "vector", "tensor_copy", ebr[:, 0:1], brows[:, 0:1])
        ew(P, "vector", "tensor_copy", ebr[:, 1:2], brows[:, 1406:1407])
        P.barrier()

    with ExitStack() as g1:
        xn_bf = sball(nc, g1, "xn_bf", [128, 8, 2048], BF16)
        xt = sball(nc, g1, "xt", [128, 8, 512], F32)
        scr = sball(nc, g1, "scr", [128, 512], F32)
        scr2 = sball(nc, g1, "scr2", [128, 512], F32)
        murow = sball(nc, g1, "murow", [1, 512], F32)
        rstdrow = sball(nc, g1, "rstdrow", [1, 512], F32)
        wkb = sball(nc, g1, "wkb", [128, 8, 128], BF16)
        vfm = sball(nc, g1, "vfm", [128, 8, 512], BF16)
        vtr = sball(nc, g1, "vtr", [128, 8, 128], BF16)

        subs = [(0, 512), (512, 512), (1024, 512), (1536, 512)]
        emit_ln(P, C, ps,
                lambda kc, t0, w, dst: dma(P, dst, fmv(xin, kc, t0, t0 + w)),
                lambda kc, t0, w: xn_bf[:, kc, t0:t0 + w],
                subs, xt, scr, scr2, murow, rstdrow)

        for which, base in (("q", 0), ("k", 1024)):
            for oc in range(8):
                for dc in range(8):
                    dma_cast(P, wkb[:, dc, :],
                             wqkvd.rearrange("(c p) o -> c p o", p=128)
                             [dc, :, base + oc * 128:base + (oc + 1) * 128])
                P.barrier()
                for tc in range(4):
                    bank = ps[tc % 2]
                    for dc in range(8):
                        mm(P, bank[:, :], wkb[:, dc, :], xn_bf[:, dc, tc * 512:(tc + 1) * 512],
                           start=(dc == 0), stop=(dc == 7))
                    P.barrier()
                    if which == "q":
                        ew(P, "vector", "tensor_copy", q_bf[:, oc, tc * 512:(tc + 1) * 512],
                           bank[:, :])
                        P.barrier()
                    else:
                        ew(P, "vector", "tensor_copy", scr[:, :].bitcast(BF16)[:, 0:512],
                           bank[:, :])
                        P.barrier()
                        dma(P, cck_in.ap().rearrange("(c p) t -> c p t", p=128)
                            [oc, :, tc * 512:(tc + 1) * 512], scr[:, :].bitcast(BF16)[:, 0:512])
                        P.barrier()
        for which, base, ccdst in (("v", 2048, ccv_in), ("kt", 1024, cckt_in)):
            for tc in range(4):
                for oc in range(8):
                    for dc in range(8):
                        dma_cast(P, wkb[:, dc, :],
                                 wqkvd.rearrange("(c p) o -> c p o", p=128)
                                 [dc, :, base + oc * 128:base + (oc + 1) * 128])
                    P.barrier()
                    bank = ps[oc % 2]
                    for dc in range(8):
                        mm(P, bank[:, :], wkb[:, dc, :], xn_bf[:, dc, tc * 512:(tc + 1) * 512],
                           start=(dc == 0), stop=(dc == 7))
                    P.barrier()
                    ew(P, "vector", "tensor_copy", vfm[:, oc, :], bank[:, :])
                    P.barrier()
                for tt in range(4):
                    for oc in range(8):
                        tr(P, pT[:, oc * 128:(oc + 1) * 128],
                           vfm[:, oc, tt * 128:(tt + 1) * 128], C.identb)
                    P.barrier()
                    for oc in range(8):
                        ew(P, "vector", "tensor_copy", vtr[:, oc, :],
                           pT[:, oc * 128:(oc + 1) * 128])
                    P.barrier()
                    dma(P, ccdst.ap()[tc * 512 + tt * 128:tc * 512 + (tt + 1) * 128, :],
                        vtr.rearrange("p a b -> p (a b)"))
                    P.barrier()

    coll(P, "AllGather", ALU.bypass, [cck_in.ap().opt()], [cck_out.ap().opt()])
    P.barrier()
    coll(P, "AllGather", ALU.bypass, [cckt_in.ap().opt()], [cckt_out.ap().opt()])
    P.barrier()
    coll(P, "AllGather", ALU.bypass, [ccv_in.ap().opt()], [ccv_out.ap().opt()])
    P.barrier()

    with ExitStack() as g3:
        khb = sball(nc, g3, "khb", [64, 16, 512], BF16)
        vhb = sball(nc, g3, "vhb", [128, 64, 65], BF16)
        kbf = sball(nc, g3, "kbf", [128, 8, 2432], BF16)
        vbt = sball(nc, g3, "vbt", [128, 19, 1024], BF16)
        mast = sball(nc, g3, "mast", [128, 10, 128], BF16)
        b2t10 = sball(nc, g3, "b2t10", [128, 5, 128], F32)
        ecls = sball(nc, g3, "ecls", [128, 64], F32)
        bcr = sball(nc, g3, "bcr", [1, 64], F32)
        bcr2 = sball(nc, g3, "bcr2", [1, 64], F32)
        lband = sball(nc, g3, "lband", [1, 512], F32)
        pbf4 = sball(nc, g3, "pbf4", [128, 4, 512], BF16)
        qh = sball(nc, g3, "qh", [64, 2048], BF16)
        kbh = sball(nc, g3, "kbh", [64, 2432], BF16)
        ofm = sball(nc, g3, "ofm", [64, 512], F32)
        lcol = sball(nc, g3, "lcol", [1, 2], F32)
        ps_o = ps[4]
        ps_l = ps[5]

        ew(P, "vector", "memset", vhb[:, :, 64:65], 1.0)
        cckt_flat = cckt_out.ap().rearrange("a b d -> (a b) d")
        ccv_flat = ccv_out.ap().rearrange("a b d -> (a b) d")
        with ExitStack() as g3b:
            kbt = pbf4[:, 0:2, :].rearrange("p a b -> p (a b)")
            for w in range(19):
                idma(P, kbt[:, :], cckt_flat, bix[:, w:w + 1])
                idma(P, vbt[:, w, :], ccv_flat, bix[:, w:w + 1])
                P.barrier()
                for oc in range(8):
                    tr(P, pT[:, oc * 128:(oc + 1) * 128],
                       kbt[:, oc * 128:(oc + 1) * 128], C.identb)
                P.barrier()
                for oc in range(8):
                    ew(P, "vector", "tensor_copy", kbf[:, oc, w * 128:(w + 1) * 128],
                       pT[:, oc * 128:(oc + 1) * 128])
                P.barrier()

        for h in range(16):
            hc, ho = h // 2, (h % 2) * 64
            ew(P, "vector", "tensor_copy", qh[:, :], q_bf[ho:ho + 64, hc, :])
            ew(P, "vector", "tensor_copy", kbh[:, :], kbf[ho:ho + 64, hc, :])
            for r in range(4):
                dma(P, khb[:, 4 * r:4 * r + 4, :],
                    cck_out.ap()[r, h * 64:(h + 1) * 64, :]
                    .rearrange("p (c t) -> p c t", t=512))
                for cc2 in range(8):
                    dma(P, vhb[:, 16 * r + 2 * cc2:16 * r + 2 * cc2 + 2, 0:64],
                        ccv_out.ap()[r].rearrange("(c p) d -> p c d", p=128)
                        [:, 2 * cc2:2 * cc2 + 2, h * 64:(h + 1) * 64])
            P.barrier()
            for half in range(2):
                mis = list(range(5 * half, 5 * half + 5))
                for k5, mi in enumerate(mis):
                    off = 512 - (mi - 5) * 128
                    srcap = RawAP(tensor=rband_h, offset=h * 1407 + off, ap=[[1, 128], [1, 128]])
                    dma(P, b2t10[:, k5, :], srcap)
                P.barrier()
                for k5, mi in enumerate(mis):
                    bank, bo = (ps[2], 0) if k5 < 4 else (ps[6], 4)
                    mm(P, bank[:, (k5 - bo) * 128:(k5 - bo + 1) * 128], C.antidiag, b2t10[:, k5, :])
                P.barrier()
                for k5, mi in enumerate(mis):
                    bank, bo = (ps[2], 0) if k5 < 4 else (ps[6], 4)
                    ew(P, "vector", "tensor_copy", mast[:, mi, :],
                       bank[:, (k5 - bo) * 128:(k5 - bo + 1) * 128])
                P.barrier()
            mm(P, ps[2][0:1, 0:1], C.bsel16[:, h * 128:h * 128 + 1], ebr[0:16, 0:1])
            mm(P, ps[2][0:1, 1:2], C.bsel16[:, h * 128:h * 128 + 1], ebr[0:16, 1:2])
            P.barrier()
            ew(P, "vector", "tensor_copy", lcol[0:1, 0:2], ps[2][0:1, 0:2])
            P.barrier()
            for qg in range(4):
                fsl = slice(qg * 64, (qg + 1) * 64)
                ew(P, "vector", "tensor_scalar", bcr[:, :], fRt[0:1, fsl], lcol[0:1, 0:1],
                   None, ALU.mult, ALU.bypass)
                ew(P, "vector", "tensor_scalar", bcr2[:, :], fLt[0:1, fsl], lcol[0:1, 1:2],
                   None, ALU.mult, ALU.bypass)
                ew(P, "vector", "tensor_add", bcr[:, :], bcr[:, :], bcr2[:, :])
                ew(P, "vector", "tensor_scalar", bcr2[:, :], fXt[0:1, fsl], NEG,
                   -25.0, ALU.mult, ALU.add)
                ew(P, "vector", "tensor_add", bcr[:, :], bcr[:, :], bcr2[:, :])
                P.barrier()
                mm(P, ps[2][:, 0:64], C.ones1f[0:1, 0:128], bcr[0:1, :])
                P.barrier()
                ew(P, "vector", "tensor_copy", ecls[:, :], ps[2][:, 0:64])
                P.barrier()
                qsl2 = slice(qg * 512, (qg + 1) * 512)
                qrhs = qh[:, qsl2]
                nacc = 0

                def khsl(kt):
                    return khb[:, kt // 4, (kt % 4) * 128:(kt % 4) * 128 + 128]

                for s in range(33):
                    if s < 32:
                        for i in range(2):
                            kt = 2 * s + i
                            mm(P, ps[(s % 2) * 2 + i][:, :], khsl(kt), qrhs)
                    if 1 <= s <= 32:
                        sp = s - 1
                        for i in range(2):
                            kt = 2 * sp + i
                            act(P, pbf4[:, (sp % 2) * 2 + i, :], ps[(sp % 2) * 2 + i][:, :],
                                AF.Exp, bias=ecls[:, kt:kt + 1])
                    if s >= 2:
                        sq = s - 2
                        for i in range(2):
                            kt = 2 * sq + i
                            mm(P, ps_o[0:65, 0:512], vhb[:, kt, :], pbf4[:, (sq % 2) * 2 + i, :],
                               start=(nacc == 0), stop=False)
                            nacc += 1
                    P.barrier()
                for w7 in range(7):
                    wreg = qg * 4 + w7
                    bank = ps[w7 % 2]
                    mm(P, bank[:, :], kbh[:, wreg * 128:(wreg + 1) * 128], qrhs,
                       start=True, stop=False)
                    for qt in range(4):
                        mi = (w7 - 2 - qt) + 5
                        mm(P, bank[:, qt * 128:(qt + 1) * 128], C.identb, mast[:, mi, :],
                           start=False, stop=(qt == 3))
                    P.barrier()
                    act(P, pbf4[:, 0, :], bank[:, :], AF.Exp,
                        bias=emk[:, qg * 7 + w7:qg * 7 + w7 + 1])
                    P.barrier()
                    last = (w7 == 6)
                    mm(P, ps_l[0:1, 0:512], C.onescb[:, 0:1], pbf4[:, 0, :],
                       start=(w7 == 0), stop=last)
                    mm(P, ps_o[0:64, 0:512], vbt[:, wreg, h * 64:(h + 1) * 64], pbf4[:, 0, :],
                       start=False, stop=False)
                    P.barrier()
                mm(P, ps_o[0:65, 0:512], vhb[:, 62, :], pbf4[:, 2, :],
                   start=False, stop=False)
                mm(P, ps_o[0:65, 0:512], vhb[:, 63, :], pbf4[:, 3, :],
                   start=False, stop=True)
                P.barrier()
                ew(P, "vector", "tensor_copy", lband[0:1, :], ps_l[0:1, 0:512])
                ew(P, "vector", "tensor_add", lband[0:1, :], ps_o[64:65, 0:512], lband[0:1, :])
                ew(P, "vector", "reciprocal", lband[0:1, :], lband[0:1, :])
                P.barrier()
                mm(P, ps[3][0:64, 0:512], C.ones1f[0:1, 0:64], lband[0:1, :])
                P.barrier()
                ew(P, "vector", "tensor_copy", ofm[:, :], ps_o[0:64, 0:512])
                P.barrier()
                ew(P, "vector", "tensor_mul", o_bf[ho:ho + 64, hc, qsl2], ofm[:, :],
                   ps[3][0:64, 0:512])
                P.barrier()

    with ExitStack() as g4:
        wob = sball(nc, g4, "wob", [128, 8, 128], BF16)
        xrc = sball(nc, g4, "xrc", [128, 512], F32)
        ob = sball(nc, g4, "ob", [128, 512], F32)
        for oc in range(8):
            for dc in range(8):
                dma_cast(P, wob[:, dc, :],
                         wod.rearrange("(c p) o -> c p o", p=128)[dc, :, oc * 128:(oc + 1) * 128])
            P.barrier()
            for tc in range(4):
                bank = ps[tc % 2]
                for dc in range(8):
                    mm(P, bank[:, :], wob[:, dc, :], o_bf[:, dc, tc * 512:(tc + 1) * 512],
                       start=(dc == 0), stop=(dc == 7))
                P.barrier()
                dma(P, xrc[:, :], fmv(xin, oc, tc * 512, (tc + 1) * 512))
                P.barrier()
                ew(P, "vector", "tensor_add", ob[:, :], bank[:, :], xrc[:, :])
                P.barrier()
                dma(P, fmv(xo, oc, tc * 512, (tc + 1) * 512), ob[:, :])
                P.barrier()
    P.emit(nc, None)
    return nc


def stage_l1attnb(d, xcur):
    wqkv = np.ascontiguousarray(d["L1_attn_b_W_qkv"].astype(np.float32))
    wo = np.ascontiguousarray(d["L1_attn_b_W_out"].astype(np.float32))
    rel = np.ascontiguousarray(d["L1_attn_b_rel_emb"].astype(np.float32))
    ind2 = relpos_indicator()
    maps = []
    for core in range(8):
        g, qc = core // 4, core % 4
        q0 = qc * 2048
        xin = np.ascontiguousarray(xcur[g, q0:q0 + 2048].T).astype(np.float32)
        bandidx = np.zeros((128, 19), np.int32)
        emask2 = np.zeros((128, 28), np.float32)
        for w in range(19):
            toks = q0 - 256 + w * 128 + np.arange(128)
            bandidx[:, w] = np.clip(toks, 0, 8191)
        for qg in range(4):
            for w7 in range(7):
                toks = q0 - 256 + (qg * 4 + w7) * 128 + np.arange(128)
                emask2[:, qg * 7 + w7] = np.where((toks >= 0) & (toks < 8192), -25.0, NEG)
        fL = np.zeros((1, 256), np.float32)
        fR = np.zeros((1, 256), np.float32)
        fX = np.zeros((1, 256), np.float32)
        for qg in range(4):
            wstart = (q0 + qg * 512 - 256) // 128
            wtiles = set(range(wstart, wstart + 7))
            for kt in range(64):
                idx = qg * 64 + kt
                if kt in wtiles:
                    fX[0, idx] = 1.0
                elif kt < wstart:
                    fL[0, idx] = 1.0
                else:
                    fR[0, idx] = 1.0
        m = dict(xin=xin, wqkv=wqkv, wo=wo, rel=rel, ind2=ind2,
                 fL=fL, fR=fR, fX=fX, emask2=emask2, bandidx=bandidx)
        m.update(const_inputs())
        maps.append(m)
    return maps


def build_l1ffn():
    nc = mknc()
    xin = nc.dram_tensor("xin", [1024, 2048], F32, kind="ExternalInput").ap()
    rwd = nc.dram_tensor("rw_r", [128, 8, 8], F32, kind="ExternalInput").ap()
    rbd = nc.dram_tensor("rb", [8, 1], F32, kind="ExternalInput").ap()
    w1d = nc.dram_tensor("w1", [8, 1024, 8192], F32, kind="ExternalInput").ap()
    b1d = nc.dram_tensor("b1_r", [128, 8, 64], F32, kind="ExternalInput").ap()
    w2d = nc.dram_tensor("w2", [8, 4096, 1024], F32, kind="ExternalInput").ap()
    b2d = nc.dram_tensor("b2_r", [128, 8, 8], F32, kind="ExternalInput").ap()
    xo = nc.dram_tensor("xo", [1024, 2048], F32, kind="ExternalOutput").ap()

    gx_h = nc.dram_tensor("gx", [4096, 1024], BF16)
    gy_h = nc.dram_tensor("gy", [4096, 1024], BF16)
    gx, gy = gx_h.ap(), gy_h.ap()

    P = Prog()
    ps = alloc_psums(nc)
    pT = ps[7]
    C = Consts(nc, P)
    rwt = nc.alloc_sbuf_tensor("rwt", [128, 8, 8], BF16).ap()
    rbt = nc.alloc_sbuf_tensor("rbt", [8, 1], F32).ap()
    b1t = nc.alloc_sbuf_tensor("b1t", [128, 8, 64], F32).ap()
    b2t2 = nc.alloc_sbuf_tensor("b2t2", [128, 8, 8], F32).ap()
    slots = nc.alloc_sbuf_tensor("slots", [128, 16], I32).ap()
    dma_cast(P, rwt[:, :, :], rwd[:, :, :])
    dma(P, rbt[:, :], rbd[:, :])
    dma(P, b1t[:, :, :], b1d[:, :, :])
    dma(P, b2t2[:, :, :], b2d[:, :, :])
    P.barrier()

    with ExitStack() as g1:
        xn_bf = sball(nc, g1, "xn_bf", [128, 8, 2048], BF16)
        xt = sball(nc, g1, "xt", [128, 8, 512], F32)
        scr = sball(nc, g1, "scr", [128, 512], F32)
        scr2 = sball(nc, g1, "scr2", [128, 512], F32)
        murow = sball(nc, g1, "murow", [1, 512], F32)
        rstdrow = sball(nc, g1, "rstdrow", [1, 512], F32)
        lrows = sball(nc, g1, "lrows", [8, 2048], F32)
        ltok = sball(nc, g1, "ltok", [128, 8], F32)
        ohb = sball(nc, g1, "ohb", [128, 8], BF16)
        mrow = sball(nc, g1, "mrow", [128, 1], F32)
        srow = sball(nc, g1, "srow", [128, 1], F32)
        brow = sball(nc, g1, "brow", [128, 1], F32)
        carry = sball(nc, g1, "carry", [1, 8], F32)
        ctok = sball(nc, g1, "ctok", [128, 8], F32)
        sltf = sball(nc, g1, "sltf", [128, 1], F32)
        xtok = sball(nc, g1, "xtok", [128, 8, 128], BF16)

        subs = [(0, 512), (512, 512), (1024, 512), (1536, 512)]
        emit_ln(P, C, ps,
                lambda kc, t0, w, dst: dma(P, dst, fmv(xin, kc, t0, t0 + w)),
                lambda kc, t0, w: xn_bf[:, kc, t0:t0 + w],
                subs, xt, scr, scr2, murow, rstdrow)

        for tc in range(4):
            for kc in range(8):
                mm(P, ps[0][0:8, 0:512], rwt[:, kc, :], xn_bf[:, kc, tc * 512:(tc + 1) * 512],
                   start=(kc == 0), stop=(kc == 7))
            P.barrier()
            ew(P, "vector", "tensor_scalar_add", lrows[:, tc * 512:(tc + 1) * 512],
               ps[0][0:8, 0:512], rbt[:, 0:1])
            P.barrier()
        ew(P, "gpsimd", "memset", carry[:, :], 0.0)
        P.barrier()
        for c in range(16):
            tr(P, ps[0][:, 0:8], lrows[:, c * 128:(c + 1) * 128], C.identf[0:8, 0:8])
            P.barrier()
            ew(P, "vector", "tensor_copy", ltok[:, :], ps[0][:, 0:8])
            P.barrier()
            ew(P, "vector", "reduce_max", mrow[:, :], ltok[:, :], AX.X)
            P.barrier()
            ew(P, "vector", "tensor_scalar", ohb[:, :], ltok[:, :], mrow[:, 0:1],
               None, ALU.is_ge, ALU.bypass)
            P.barrier()
            mm(P, ps[1][:, 0:8], C.triu_onesb[:, :], ohb[:, :], start=True, stop=False)
            mm(P, ps[1][:, 0:8], C.ones1f[0:1, 0:128], carry[0:1, :], start=False, stop=True)
            P.barrier()
            ew(P, "vector", "tensor_copy", ctok[:, :], ps[1][:, 0:8])
            P.barrier()
            mm(P, ps[2][0:1, 0:8], C.sel127[:, 0:1], ctok[:, :])
            P.barrier()
            ew(P, "vector", "tensor_copy", carry[0:1, :], ps[2][0:1, 0:8])
            ew(P, "vector", "tensor_mul", scr[:, 0:8], ctok[:, :], ohb[:, :])
            P.barrier()
            ew(P, "vector", "reduce_sum", srow[:, :], scr[:, 0:8], AX.X)
            P.barrier()
            ew(P, "vector", "tensor_mul", scr[:, 0:8], C.e8row[:, :], ohb[:, :])
            P.barrier()
            ew(P, "vector", "reduce_sum", brow[:, :], scr[:, 0:8], AX.X)
            P.barrier()
            ew(P, "vector", "tensor_scalar", sltf[:, :], brow[:, :], 512.0, -1.0,
               ALU.mult, ALU.add)
            P.barrier()
            ew(P, "vector", "tensor_add", sltf[:, :], sltf[:, :], srow[:, :])
            P.barrier()
            ew(P, "vector", "tensor_scalar", scr[:, 0:1], srow[:, :], 512.5, 1e9,
               ALU.is_ge, ALU.mult)
            P.barrier()
            ew(P, "vector", "tensor_add", sltf[:, :], sltf[:, :], scr[:, 0:1])
            P.barrier()
            ew(P, "vector", "tensor_copy", slots[:, c:c + 1], sltf[:, :])
            P.barrier()
            for oc in range(8):
                tr(P, pT[:, oc * 128:(oc + 1) * 128], xn_bf[:, oc, c * 128:(c + 1) * 128],
                   C.identb)
            P.barrier()
            for oc in range(8):
                ew(P, "vector", "tensor_copy", xtok[:, oc, :], pT[:, oc * 128:(oc + 1) * 128])
            P.barrier()
            P.op("gpsimd", lambda e, d=gx, i=slots[:, c:c + 1], s=xtok:
                 e.indirect_dma_start(out=d, out_offset=IndirectOffsetOnAxis(i, 0),
                                      in_=s.rearrange("p a b -> p (a b)"),
                                      in_offset=None), dma=True)
            P.barrier()

    with ExitStack() as g2:
        xg_tok = sball(nc, g2, "xg_tok", [128, 4, 1024], BF16)
        xg_fm = sball(nc, g2, "xg_fm", [128, 8, 512], BF16)
        w1b = sball(nc, g2, "w1b", [128, 8, 512], BF16)
        w1g = sball(nc, g2, "w1g", [128, 8, 512], BF16)
        sgb = sball(nc, g2, "sgb", [128, 512], F32)
        ab = sball(nc, g2, "ab", [128, 32, 512], BF16)
        w2b = sball(nc, g2, "w2b", [128, 32, 512], BF16)
        yfm = sball(nc, g2, "yfm", [128, 8, 512], BF16)
        ytok = sball(nc, g2, "ytok", [128, 4, 1024], BF16)
        for e in range(8):
            dma(P, xg_tok[:, :, :],
                gx[512 * e:512 * (e + 1), :].rearrange("(a p) b -> p a b", p=128))
            P.barrier()
            for a4 in range(4):
                for oc in range(8):
                    tr(P, pT[:, oc * 128:(oc + 1) * 128],
                       xg_tok[:, a4, oc * 128:(oc + 1) * 128], C.identb)
                P.barrier()
                for oc in range(8):
                    ew(P, "vector", "tensor_copy", xg_fm[:, oc, a4 * 128:(a4 + 1) * 128],
                       pT[:, oc * 128:(oc + 1) * 128])
                P.barrier()
            for og in range(8):
                for dc in range(8):
                    dma_cast(P, w1b[:, dc, :],
                             w1d[e].rearrange("(c p) f -> c p f", p=128)
                             [dc, :, og * 512:(og + 1) * 512])
                    dma_cast(P, w1g[:, dc, :],
                             w1d[e].rearrange("(c p) f -> c p f", p=128)
                             [dc, :, 4096 + og * 512:4096 + (og + 1) * 512])
                P.barrier()
                for foc in range(4):
                    fg = og * 4 + foc
                    for dc in range(8):
                        mm(P, ps[0][:, :], w1g[:, dc, foc * 128:(foc + 1) * 128],
                           xg_fm[:, dc, :], start=(dc == 0), stop=(dc == 7))
                        mm(P, ps[1][:, :], w1b[:, dc, foc * 128:(foc + 1) * 128],
                           xg_fm[:, dc, :], start=(dc == 0), stop=(dc == 7))
                    P.barrier()
                    act(P, sgb[:, :], ps[0][:, :], AF.Sigmoid, bias=b1t[:, e, 32 + fg:33 + fg])
                    P.barrier()
                    ew(P, "vector", "scalar_tensor_tensor", ab[:, fg, :], ps[1][:, :],
                       b1t[:, e, fg:fg + 1], sgb[:, :], ALU.add, ALU.mult)
                    P.barrier()
            for ocg in range(2):
                for hic in range(32):
                    dma_cast(P, w2b[:, hic, :],
                             w2d[e].rearrange("(c p) o -> c p o", p=128)
                             [hic, :, ocg * 512:(ocg + 1) * 512])
                P.barrier()
                for doc in range(4):
                    do = ocg * 4 + doc
                    bank = ps[doc % 2]
                    for hic in range(32):
                        mm(P, bank[:, :], w2b[:, hic, doc * 128:(doc + 1) * 128],
                           ab[:, hic, :], start=(hic == 0), stop=(hic == 31))
                    P.barrier()
                    ew(P, "vector", "tensor_scalar_add", yfm[:, do, :], bank[:, :],
                       b2t2[:, e, do:do + 1])
                    P.barrier()
            for a4 in range(4):
                for oc in range(8):
                    tr(P, pT[:, oc * 128:(oc + 1) * 128],
                       yfm[:, oc, a4 * 128:(a4 + 1) * 128], C.identb)
                P.barrier()
                for oc in range(8):
                    ew(P, "vector", "tensor_copy", ytok[:, a4, oc * 128:(oc + 1) * 128],
                       pT[:, oc * 128:(oc + 1) * 128])
                P.barrier()
            dma(P, gy[512 * e:512 * (e + 1), :].rearrange("(a p) b -> p a b", p=128),
                ytok[:, :, :])
            P.barrier()

    with ExitStack() as g3:
        ytk = sball(nc, g3, "ytk", [128, 1024], BF16)
        ofc = sball(nc, g3, "ofc", [128, 8, 128], BF16)
        xrc = sball(nc, g3, "xrc", [128, 128], F32)
        ob = sball(nc, g3, "ob", [128, 128], F32)
        for c in range(16):
            ew(P, "gpsimd", "memset", ytk[:, :], 0.0)
            P.barrier()
            P.op("gpsimd", lambda e, d=ytk, i=slots[:, c:c + 1]:
                 e.indirect_dma_start(out=d[:, :], out_offset=None, in_=gy,
                                      in_offset=IndirectOffsetOnAxis(i, 0)), dma=True)
            P.barrier()
            for oc in range(8):
                tr(P, pT[:, oc * 128:(oc + 1) * 128], ytk[:, oc * 128:(oc + 1) * 128],
                   C.identb)
            P.barrier()
            for oc in range(8):
                ew(P, "vector", "tensor_copy", ofc[:, oc, :], pT[:, oc * 128:(oc + 1) * 128])
            P.barrier()
            for oc in range(8):
                dma(P, xrc[:, :], fmv(xin, oc, c * 128, (c + 1) * 128))
                P.barrier()
                ew(P, "vector", "tensor_add", ob[:, :], ofc[:, oc, :], xrc[:, :])
                P.barrier()
                dma(P, fmv(xo, oc, c * 128, (c + 1) * 128), ob[:, :])
                P.barrier()

    P.emit(nc, None)
    return nc


def stage_l1ffn(d, xcur):
    rw = d["L1_ffn_router_w"].astype(np.float32)
    rw_r = np.ascontiguousarray(rw.reshape(8, 128, 8).transpose(1, 0, 2))
    rb = d["L1_ffn_router_b"].astype(np.float32).reshape(8, 1)
    w1 = np.ascontiguousarray(d["L1_ffn_w1"].astype(np.float32))
    b1 = d["L1_ffn_b1"].astype(np.float32)
    b1_r = np.ascontiguousarray(b1.reshape(8, 64, 128).transpose(2, 0, 1))
    w2 = np.ascontiguousarray(d["L1_ffn_w2"].astype(np.float32))
    b2 = d["L1_ffn_b2"].astype(np.float32)
    b2_r = np.ascontiguousarray(b2.reshape(8, 8, 128).transpose(2, 0, 1))
    maps = []
    for core in range(8):
        g, qc = core // 4, core % 4
        xin = np.ascontiguousarray(xcur[g, qc * 2048:(qc + 1) * 2048].T).astype(np.float32)
        m = dict(xin=xin, rw_r=rw_r, rb=rb, w1=w1, b1_r=b1_r, w2=w2, b2_r=b2_r)
        m.update(const_inputs())
        maps.append(m)
    return maps


def build_l0attnb():
    nc = mknc()
    xin = nc.dram_tensor("xin", [1024, 2048], F32, kind="ExternalInput").ap()
    wqd = nc.dram_tensor("wq", [1024, 1024], F32, kind="ExternalInput").ap()
    wkd = nc.dram_tensor("wk", [1024, 1024], F32, kind="ExternalInput").ap()
    wvd = nc.dram_tensor("wv", [1024, 1024], F32, kind="ExternalInput").ap()
    wod = nc.dram_tensor("wo", [1024, 1024], F32, kind="ExternalInput").ap()
    wbetd = nc.dram_tensor("wbet", [1024, 16], F32, kind="ExternalInput").ap()
    wfd = nc.dram_tensor("wf", [1024, 16], F32, kind="ExternalInput").ap()
    nwd = nc.dram_tensor("nw_r", [128, 8], F32, kind="ExternalInput").ap()
    hqd = nc.dram_tensor("hq_idx", [128, 6], I32, kind="ExternalInput").ap()
    abd = nc.dram_tensor("ab_idx", [8, 1], I32, kind="ExternalInput").ap()
    xo = nc.dram_tensor("xo", [1024, 2048], F32, kind="ExternalOutput").ap()

    cc_in = nc.dram_tensor("cc_in", [3072, 2048], BF16)
    cc_out = nc.dram_tensor("cc_out", [4, 3072, 2048], BF16)
    ccd_in = nc.dram_tensor("ccd_in", [32, 2048], F32)
    ccd_out = nc.dram_tensor("ccd_out", [4, 32, 2048], F32)
    ccy_in = nc.dram_tensor("ccy_in", [256, 8192], BF16)
    ccy_out = nc.dram_tensor("ccy_out", [4, 256, 8192], BF16)

    P = Prog()
    ps = alloc_psums(nc)
    pT = ps[7]
    C = Consts(nc, P)
    nwt = nc.alloc_sbuf_tensor("nwt", [128, 8], F32).ap()
    hqi = nc.alloc_sbuf_tensor("hqi", [128, 6], I32).ap()
    abi = nc.alloc_sbuf_tensor("abi", [8, 1], I32).ap()
    dma(P, nwt[:, :], nwd[:, :])
    dma(P, hqi[:, :], hqd[:, :])
    dma(P, abi[:, :], abd[:, :])
    P.barrier()

    with ExitStack() as g1:
        xn_bf = sball(nc, g1, "xn_bf", [128, 8, 2048], BF16)
        xt = sball(nc, g1, "xt", [128, 8, 512], F32)
        scr = sball(nc, g1, "scr", [128, 512], F32)
        scr2 = sball(nc, g1, "scr2", [128, 512], F32)
        murow = sball(nc, g1, "murow", [1, 512], F32)
        rstdrow = sball(nc, g1, "rstdrow", [1, 512], F32)
        wwb = sball(nc, g1, "wwb", [128, 8, 128], BF16)
        kfm = sball(nc, g1, "kfm", [128, 8, 2048], BF16)
        ksq = sball(nc, g1, "ksq", [16, 512], F32)
        prow = sball(nc, g1, "prow", [128, 512], BF16)
        abrow = sball(nc, g1, "abrow", [16, 512], F32)

        subs = [(0, 512), (512, 512), (1024, 512), (1536, 512)]
        emit_ln(P, C, ps,
                lambda kc, t0, w, dst: dma(P, dst, fmv(xin, kc, t0, t0 + w)),
                lambda kc, t0, w: xn_bf[:, kc, t0:t0 + w],
                subs, xt, scr, scr2, murow, rstdrow)

        for which, wsrc, base, dosilu in (("q", wqd, 0, True), ("v", wvd, 2048, False)):
            for oc in range(8):
                for dc in range(8):
                    dma_cast(P, wwb[:, dc, :],
                             wsrc.rearrange("(c p) o -> c p o", p=128)[dc, :, oc * 128:(oc + 1) * 128])
                P.barrier()
                for tc in range(4):
                    bank = ps[tc % 2]
                    for dc in range(8):
                        mm(P, bank[:, :], wwb[:, dc, :], xn_bf[:, dc, tc * 512:(tc + 1) * 512],
                           start=(dc == 0), stop=(dc == 7))
                    P.barrier()
                    if dosilu:
                        act(P, prow[:, :], bank[:, :], AF.Silu)
                    else:
                        ew(P, "vector", "tensor_copy", prow[:, :], bank[:, :])
                    P.barrier()
                    dma(P, cc_in.ap().rearrange("(c p) t -> c p t", p=128)
                        [base // 128 + oc, :, tc * 512:(tc + 1) * 512], prow[:, :])
                    P.barrier()
        for oc in range(8):
            for dc in range(8):
                dma_cast(P, wwb[:, dc, :],
                         wkd.rearrange("(c p) o -> c p o", p=128)[dc, :, oc * 128:(oc + 1) * 128])
            P.barrier()
            for tc in range(4):
                bank = ps[tc % 2]
                for dc in range(8):
                    mm(P, bank[:, :], wwb[:, dc, :], xn_bf[:, dc, tc * 512:(tc + 1) * 512],
                       start=(dc == 0), stop=(dc == 7))
                P.barrier()
                act(P, kfm[:, oc, tc * 512:(tc + 1) * 512], bank[:, :], AF.Silu)
                P.barrier()
        for tc in range(4):
            tsl = slice(tc * 512, (tc + 1) * 512)
            for kc in range(8):
                s = scr if kc % 2 == 0 else scr2
                act(P, s[:, :], kfm[:, kc, tsl], AF.Square)
                P.barrier()
                mm(P, ps[2][0:16, 0:512], C.hsel16c[:, kc * 16:kc * 16 + 16], s[:, :],
                   start=(kc == 0), stop=(kc == 7))
            P.barrier()
            act(P, abrow[:, :], ps[2][0:16, 0:512], AF.Ln, bias=1e-12)
            P.barrier()
            act(P, ksq[:, :], abrow[:, :], AF.Exp, scale=-0.5)
            P.barrier()
            for kc in range(8):
                h0, h1 = 2 * kc, 2 * kc + 1
                mm(P, ps[3][0:64, 0:512], C.bsel16[:, h0 * 128:h0 * 128 + 64], ksq[0:16, :])
                mm(P, ps[3][64:128, 0:512], C.bsel16[:, h1 * 128:h1 * 128 + 64], ksq[0:16, :])
                P.barrier()
                ew(P, "vector", "tensor_mul", prow[:, :], kfm[:, kc, tsl], ps[3][:, 0:512])
                P.barrier()
                dma(P, cc_in.ap().rearrange("(c p) t -> c p t", p=128)
                    [8 + kc, :, tsl], prow[:, :])
                P.barrier()
        for which, wsrc, robase, two in (("a", wfd, 0, False), ("b", wbetd, 16, True)):
            with ExitStack() as gw:
                wsb = sball(nc, gw, "wsb_" + which, [128, 8, 16], BF16)
                for dc in range(8):
                    dma_cast(P, wsb[:, dc, :],
                             wsrc.rearrange("(c p) o -> c p o", p=128)[dc, :, :])
                P.barrier()
                for tc in range(4):
                    for dc in range(8):
                        mm(P, ps[0][0:16, 0:512], wsb[:, dc, :],
                           xn_bf[:, dc, tc * 512:(tc + 1) * 512],
                           start=(dc == 0), stop=(dc == 7))
                    P.barrier()
                    act(P, abrow[:, :], ps[0][0:16, 0:512], AF.Sigmoid)
                    P.barrier()
                    if two:
                        ew(P, "vector", "tensor_scalar_mul", abrow[:, :], abrow[:, :], 2.0)
                        P.barrier()
                    dma(P, ccd_in.ap()[robase:robase + 16, tc * 512:(tc + 1) * 512], abrow[:, :])
                    P.barrier()

    coll(P, "AllGather", ALU.bypass, [cc_in.ap().opt()], [cc_out.ap().opt()])
    P.barrier()
    coll(P, "AllGather", ALU.bypass, [ccd_in.ap().opt()], [ccd_out.ap().opt()])
    P.barrier()

    ccr_in = nc.dram_tensor("ccr_in", [4096, 2048], F32)
    ccr_out = nc.dram_tensor("ccr_out", [1024, 2048], F32)
    nwhd = nc.dram_tensor("nwh_d", [128, 2], F32, kind="ExternalInput").ap()
    wo256d = nc.dram_tensor("wo256", [256, 1024], F32, kind="ExternalInput").ap()
    nwh = nc.alloc_sbuf_tensor("nwh", [128, 2], F32).ap()
    dma(P, nwh[:, :], nwhd[:, :])
    P.barrier()

    with ExitStack() as g2:
        qh4 = sball(nc, g2, "qh4", [128, 2, 2048], BF16)
        kh4 = sball(nc, g2, "kh4", [128, 2, 2048], BF16)
        vh4 = sball(nc, g2, "vh4", [128, 2, 2048], BF16)
        ab8 = sball(nc, g2, "ab8", [8, 2048], F32)
        lna = sball(nc, g2, "lna", [4, 2048], F32)
        lgs = sball(nc, g2, "lgs", [4, 2048], F32)
        nlg = sball(nc, g2, "nlg", [4, 2048], F32)
        eg4 = sball(nc, g2, "eg4", [4, 128], F32)
        egr2 = sball(nc, g2, "egr2", [4, 128], F32)
        trb = sball(nc, g2, "trb", [8, 128], F32)
        tcols = sball(nc, g2, "tcols", [128, 8], F32)
        gcol4 = sball(nc, g2, "gcol4", [128, 4], F32)
        gTe = sball(nc, g2, "gTe", [4, 1], F32)
        gTc = sball(nc, g2, "gTc", [64, 4], F32)
        ones4 = sball(nc, g2, "ones4", [4, 128], F32)
        khh = sball(nc, g2, "khh", [64, 128], BF16)
        qhh = sball(nc, g2, "qhh", [64, 128], BF16)
        vhh = sball(nc, g2, "vhh", [64, 128], BF16)
        kbm = sball(nc, g2, "kbm", [64, 128], BF16)
        kbp = sball(nc, g2, "kbp", [64, 128], BF16)
        g2s = sball(nc, g2, "g2s", [128, 128], F32)
        g2ts = sball(nc, g2, "g2ts", [128, 128], F32)
        gas = sball(nc, g2, "gas", [128, 128], F32)
        dsb3 = sball(nc, g2, "dsb3", [128, 384], F32)
        negM = sball(nc, g2, "negM", [128, 128], BF16)
        negMT = sball(nc, g2, "negMT", [128, 128], BF16)
        negAT = sball(nc, g2, "negAT", [128, 128], BF16)
        Mp = sball(nc, g2, "Mp", [128, 128], BF16)
        MpT = sball(nc, g2, "MpT", [128, 128], BF16)
        Xb = sball(nc, g2, "Xb", [128, 128], BF16)
        AXb = sball(nc, g2, "AXb", [128, 128], BF16)
        Wtn = sball(nc, g2, "Wtn", [64, 128], BF16)
        Zb = sball(nc, g2, "Zb", [64, 128], BF16)
        qhat = sball(nc, g2, "qhat", [64, 128], F32)
        Ub = sball(nc, g2, "Ub", [128, 64], BF16)
        kbart = sball(nc, g2, "kbart", [128, 64], BF16)
        S0 = sball(nc, g2, "S0", [64, 4, 64], F32)
        S0b = sball(nc, g2, "S0b", [64, 64], BF16)
        hscl = sball(nc, g2, "hscl", [64, 64], F32)
        osq = sball(nc, g2, "osq", [128, 64], F32)
        ors1 = sball(nc, g2, "ors1", [128, 1], F32)
        ors2 = sball(nc, g2, "ors2", [128, 1], F32)
        obf = sball(nc, g2, "obf", [128, 64], BF16)
        ofm4 = sball(nc, g2, "ofm4", [128, 2, 8192], BF16)

        ew(P, "gpsimd", "memset", S0[:, :, :], 0.0)
        ew(P, "gpsimd", "memset", ones4[:, :], 1.0)
        P.barrier()
        ccf = cc_out.ap().rearrange("a b t -> (a b) t")
        ccdf = ccd_out.ap().rearrange("a b t -> (a b) t")
        for r in range(4):
            for i in range(2):
                idma(P, qh4[:, i, :], ccf, hqi[:, i:i + 1], eoff=(r * 3072) * 2048)
                idma(P, kh4[:, i, :], ccf, hqi[:, 2 + i:3 + i], eoff=(r * 3072) * 2048)
                idma(P, vh4[:, i, :], ccf, hqi[:, 4 + i:5 + i], eoff=(r * 3072) * 2048)
            idma(P, ab8[:, :], ccdf, abi[:, 0:1], eoff=(r * 32) * 2048)
            P.barrier()
            act(P, lna[:, :], ab8[0:4, :], AF.Ln)
            P.barrier()
            for c in range(16):
                csl = slice(c * 128, (c + 1) * 128)
                ew(P, "vector", "tensor_tensor_scan", lgs[:, csl], ones4[:, :],
                   lna[:, csl], 0.0, ALU.mult, ALU.add)
            P.barrier()
            ew(P, "vector", "tensor_scalar_mul", nlg[:, :], lgs[:, :], -1.0)
            P.barrier()
            for c in range(16):
                csl = slice(c * 128, (c + 1) * 128)
                cT = c * 128 + 127
                act(P, eg4[:, :], lgs[:, csl], AF.Exp)
                ew(P, "vector", "tensor_scalar", trb[0:4, :], lgs[:, csl],
                   lgs[:, cT:cT + 1], -1.0, ALU.subtract, ALU.mult)
                act(P, gTe[:, :], lgs[:, cT:cT + 1], AF.Exp)
                P.barrier()
                act(P, egr2[:, :], trb[0:4, :], AF.Exp)
                P.barrier()
                tr(P, ps[6][:, 0:4], lgs[0:4, csl], C.identf[0:4, 0:4])
                tr(P, ps[6][:, 4:8], egr2[0:4, :], C.identf[0:4, 0:4])
                for h4 in range(4):
                    mm(P, ps[6][0:64, 16 + h4:17 + h4],
                       C.bsel8[0:4, h4 * 128:h4 * 128 + 64], gTe[:, :])
                P.barrier()
                ew(P, "vector", "tensor_copy", tcols[:, :], ps[6][:, 0:8])
                ew(P, "vector", "tensor_copy", gTc[:, :], ps[6][0:64, 16:20])
                P.barrier()
                act(P, gcol4[:, :], tcols[:, 0:4], AF.Exp)
                P.barrier()
                for h in range(4):
                    ho, hcb = (h % 2) * 64, h // 2
                    ew(P, "vector", "tensor_copy", khh[:, :], kh4[ho:ho + 64, hcb, csl])
                    ew(P, "vector", "tensor_copy", qhh[:, :], qh4[ho:ho + 64, hcb, csl])
                    ew(P, "vector", "tensor_copy", vhh[:, :], vh4[ho:ho + 64, hcb, csl])
                    ew(P, "vector", "tensor_copy", S0b[:, :], S0[:, h, :])
                    mm(P, ps[5][0:64, 0:128], C.bsel8[:, (4 + h) * 128:(4 + h) * 128 + 64],
                       ab8[0:8, csl])
                    P.barrier()
                    ew(P, "vector", "scalar_tensor_tensor", kbm[:, :], khh[:, :], -1.0,
                       ps[5][0:64, 0:128], ALU.mult, ALU.mult)
                    P.barrier()
                    ew(P, "vector", "tensor_scalar_mul", kbp[:, :], kbm[:, :], -1.0)
                    P.barrier()
                    mm(P, ps[0][:, 0:128], khh[:, :], kbm[:, :])
                    mm(P, ps[1][:, 0:128], kbm[:, :], khh[:, :])
                    mm(P, ps[2][:, 0:128], kbm[:, :], qhh[:, :])
                    P.barrier()
                    ew(P, "vector", "tensor_copy", g2s[:, :], ps[0][:, 0:128])
                    ew(P, "vector", "tensor_copy", g2ts[:, :], ps[1][:, 0:128])
                    ew(P, "vector", "tensor_copy", gas[:, :], ps[2][:, 0:128])
                    P.barrier()
                    bank = ps[0]
                    for di, msk in enumerate((C.msk_sls, C.msk_sus, C.msk_su)):
                        o = di * 128
                        if di == 0:
                            mm(P, bank[:, o:o + 128], lgs[0:4, csl],
                               C.bsel8[0:4, h * 128:(h + 1) * 128], start=True, stop=False)
                            mm(P, bank[:, o:o + 128], C.bsel8[0:4, h * 128:(h + 1) * 128],
                               nlg[0:4, csl], start=False, stop=False)
                        else:
                            mm(P, bank[:, o:o + 128], nlg[0:4, csl],
                               C.bsel8[0:4, h * 128:(h + 1) * 128], start=True, stop=False)
                            mm(P, bank[:, o:o + 128], C.bsel8[0:4, h * 128:(h + 1) * 128],
                               lgs[0:4, csl], start=False, stop=False)
                        mm(P, bank[:, o:o + 128], C.identf, msk, start=False, stop=True)
                    P.barrier()
                    act(P, dsb3[:, :], bank[:, 0:384], AF.Exp)
                    P.barrier()
                    ew(P, "vector", "tensor_mul", negM[:, :], g2s[:, :], dsb3[:, 0:128])
                    ew(P, "vector", "tensor_mul", negMT[:, :], g2ts[:, :], dsb3[:, 128:256])
                    ew(P, "vector", "tensor_mul", negAT[:, :], gas[:, :], dsb3[:, 256:384])
                    P.barrier()
                    tr(P, pT[:, 0:64], vhh[:, :], C.identb[0:64, 0:64])
                    tr(P, pT[:, 64:128], khh[:, :], C.identb[0:64, 0:64])
                    P.barrier()
                    ew(P, "vector", "tensor_copy", Xb[:, 0:64], pT[:, 0:64])
                    ew(P, "vector", "tensor_scalar_mul", Xb[:, 64:128], pT[:, 64:128],
                       gcol4[:, h:h + 1])
                    P.barrier()
                    mm(P, ps[1][:, 0:128], C.identb, Xb[:, :], start=True, stop=False)
                    mm(P, ps[1][:, 0:128], negMT[:, :], Xb[:, :], start=False, stop=True)
                    P.barrier()
                    ew(P, "vector", "tensor_copy", Xb[:, :], ps[1][:, 0:128])
                    ew(P, "vector", "tensor_copy", Mp[:, :], negM[:, :])
                    ew(P, "vector", "tensor_copy", MpT[:, :], negMT[:, :])
                    P.barrier()
                    for lev in range(6):
                        mm(P, ps[1][:, 0:128], MpT[:, :], Mp[:, :])
                        mm(P, ps[2][:, 0:128], Mp[:, :], MpT[:, :])
                        P.barrier()
                        ew(P, "vector", "tensor_copy", Mp[:, :], ps[1][:, 0:128])
                        ew(P, "vector", "tensor_copy", MpT[:, :], ps[2][:, 0:128])
                        P.barrier()
                        mm(P, ps[1][:, 0:128], C.identb, Xb[:, :], start=True, stop=False)
                        mm(P, ps[1][:, 0:128], MpT[:, :], Xb[:, :], start=False, stop=True)
                        P.barrier()
                        ew(P, "vector", "tensor_copy", Xb[:, :], ps[1][:, 0:128])
                        P.barrier()
                    mm(P, ps[1][:, 0:128], negAT[:, :], Xb[:, :])
                    tr(P, pT[0:64, 0:128], Xb[:, 64:128], C.identb)
                    P.barrier()
                    ew(P, "vector", "tensor_scalar_mul", AXb[:, :], ps[1][:, 0:128], -1.0)
                    ew(P, "vector", "tensor_scalar_mul", Wtn[:, :], pT[0:64, 0:128], -1.0)
                    P.barrier()
                    mm(P, ps[6][0:64, 0:128], C.bsel8[0:4, h * 128:h * 128 + 64], eg4[0:4, :])
                    tr(P, pT[0:64, 0:128], AXb[:, 64:128], C.identb)
                    P.barrier()
                    ew(P, "vector", "tensor_mul", qhat[:, :], qhh[:, :], ps[6][0:64, 0:128])
                    P.barrier()
                    ew(P, "vector", "tensor_sub", Zb[:, :], qhat[:, :], pT[0:64, 0:128])
                    P.barrier()
                    mm(P, ps[3][:, 0:64], C.identb, Xb[:, 0:64], start=True, stop=False)
                    mm(P, ps[3][:, 0:64], Wtn[:, :], S0b[:, :], start=False, stop=True)
                    P.barrier()
                    ew(P, "vector", "tensor_copy", Ub[:, :], ps[3][:, 0:64])
                    P.barrier()
                    mm(P, ps[4][:, 0:64], Zb[:, :], S0b[:, :], start=True, stop=False)
                    mm(P, ps[4][:, 0:64], C.identb, AXb[:, 0:64], start=False, stop=True)
                    P.barrier()
                    act(P, osq[:, :], ps[4][:, 0:64], AF.Square)
                    P.barrier()
                    ew(P, "vector", "reduce_sum", ors1[:, :], osq[:, :], AX.X)
                    P.barrier()
                    ew(P, "vector", "tensor_scalar_mul", ors1[:, :], ors1[:, :], 1.0 / 64)
                    P.barrier()
                    act(P, ors2[:, :], ors1[:, :], AF.Ln, bias=1e-6)
                    P.barrier()
                    act(P, ors1[:, :], ors2[:, :], AF.Exp, scale=-0.5)
                    P.barrier()
                    ew(P, "vector", "tensor_scalar_mul", obf[:, :], ps[4][:, 0:64],
                       ors1[:, 0:1])
                    P.barrier()
                    tr(P, pT[0:64, 0:128], obf[:, :], C.identb)
                    P.barrier()
                    ew(P, "vector", "tensor_scalar_mul",
                       ofm4[ho:ho + 64, hcb, r * 2048 + c * 128:r * 2048 + (c + 1) * 128],
                       pT[0:64, 0:128], nwh[ho:ho + 64, hcb:hcb + 1])
                    P.barrier()
                    tr(P, pT[:, 0:64], kbp[:, :], C.identb[0:64, 0:64])
                    P.barrier()
                    ew(P, "vector", "tensor_scalar_mul", kbart[:, :], pT[:, 0:64],
                       tcols[:, 4 + h:5 + h])
                    P.barrier()
                    mm(P, ps[5][0:64, 0:64], kbart[:, :], Ub[:, :])
                    P.barrier()
                    ew(P, "vector", "tensor_scalar_mul", hscl[:, :], S0[:, h, :],
                       gTc[:, h:h + 1])
                    P.barrier()
                    ew(P, "vector", "scalar_tensor_tensor", S0[:, h, :], ps[5][0:64, 0:64],
                       1.0, hscl[:, :], ALU.mult, ALU.add)
                    P.barrier()

        with ExitStack() as g5:
            wob2 = sball(nc, g5, "wob2", [128, 2, 128], BF16)
            pout = sball(nc, g5, "pout", [128, 512], F32)
            for oc in range(8):
                for kc in range(2):
                    dma_cast(P, wob2[:, kc, :],
                             wo256d.rearrange("(c p) o -> c p o", p=128)
                             [kc, :, oc * 128:(oc + 1) * 128])
                P.barrier()
                for tseg in range(16):
                    bank = ps[tseg % 2]
                    for kc in range(2):
                        mm(P, bank[:, :], wob2[:, kc, :], ofm4[:, kc, tseg * 512:(tseg + 1) * 512],
                           start=(kc == 0), stop=(kc == 1))
                    P.barrier()
                    ew(P, "vector", "tensor_copy", pout[:, :], bank[:, :])
                    P.barrier()
                    dma(P, ccr_in.ap().rearrange("(q c p) t -> q c p t", q=4, p=128)
                        [tseg // 4, oc, :, (tseg % 4) * 512:(tseg % 4 + 1) * 512], pout[:, :])
                    P.barrier()

    coll(P, "ReduceScatter", ALU.add, [ccr_in.ap().opt()], [ccr_out.ap().opt()])
    P.barrier()

    with ExitStack() as gf:
        xrc = sball(nc, gf, "xrc", [128, 512], F32)
        occ = sball(nc, gf, "occ", [128, 512], F32)
        ob = sball(nc, gf, "ob", [128, 512], F32)
        for oc in range(8):
            for tc in range(4):
                t0 = tc * 512
                dma(P, xrc[:, :], fmv(xin, oc, t0, t0 + 512))
                dma(P, occ[:, :], fmv(ccr_out.ap(), oc, t0, t0 + 512))
                P.barrier()
                ew(P, "vector", "tensor_add", ob[:, :], occ[:, :], xrc[:, :])
                P.barrier()
                dma(P, fmv(xo, oc, t0, t0 + 512), ob[:, :])
                P.barrier()

    P.emit(nc, None)
    return nc


def stage_l0attnb(d, xcur):
    wq = np.ascontiguousarray(d["L0_attn_b_Wq"].astype(np.float32))
    wk = np.ascontiguousarray(d["L0_attn_b_Wk"].astype(np.float32))
    wv = np.ascontiguousarray(d["L0_attn_b_Wv"].astype(np.float32))
    wo = np.ascontiguousarray(d["L0_attn_b_Wo"].astype(np.float32))
    wbet = np.ascontiguousarray(d["L0_attn_b_Wbet"].astype(np.float32))
    wf = np.ascontiguousarray(d["L0_attn_b_Wf"].astype(np.float32))
    nw = d["L0_attn_b_norm_w"].astype(np.float32)
    nw_r = np.ascontiguousarray(nw.reshape(8, 128).T)
    maps = []
    for core in range(8):
        g, qc = core // 4, core % 4
        xin = np.ascontiguousarray(xcur[g, qc * 2048:(qc + 1) * 2048].T).astype(np.float32)
        hq = np.zeros((128, 6), np.int32)
        for i in range(2):
            rows = qc * 256 + i * 128 + np.arange(128)
            hq[:, i] = rows
            hq[:, 2 + i] = 1024 + rows
            hq[:, 4 + i] = 2048 + rows
        ab = np.concatenate([qc * 4 + np.arange(4), 16 + qc * 4 + np.arange(4)])
        nwh = np.ascontiguousarray(nw[qc * 256:(qc + 1) * 256].reshape(2, 128).T)
        wo256 = np.ascontiguousarray(wo[qc * 256:(qc + 1) * 256, :])
        m = dict(xin=xin, wq=wq, wk=wk, wv=wv, wo=wo, wbet=wbet, wf=wf, nw_r=nw_r,
                 hq_idx=hq, ab_idx=ab.reshape(8, 1).astype(np.int32),
                 nwh_d=nwh, wo256=wo256)
        m.update(const_inputs())
        maps.append(m)
    return maps


def stage_mamba(d, direction, xcur):
    fwd = direction == "fwd"
    pre = "fwd_" if fwd else "bwd_"
    wiT = np.ascontiguousarray(d[f"L0_attn_a_{pre}W_in"].T).astype(np.float32)
    convw = d[f"L0_attn_a_{pre}conv_w"].reshape(18, 128, 4).transpose(1, 0, 2)
    convw = np.ascontiguousarray(convw.reshape(128, 72)).astype(np.float32)
    convb = np.ascontiguousarray(
        d[f"L0_attn_a_{pre}conv_b"].reshape(18, 128).T).astype(np.float32)
    dtb = d[f"L0_attn_a_{pre}dt_bias"].reshape(32, 1).astype(np.float32)
    alog = d[f"L0_attn_a_{pre}A_log"].astype(np.float32)
    Dv = d[f"L0_attn_a_{pre}D"].astype(np.float32)
    wout = d[f"L0_attn_a_{pre}W_out"].astype(np.float32)
    projT = np.ascontiguousarray(d["L0_attn_a_proj_W"].T).astype(np.float32)
    projTd = np.ascontiguousarray(projT[0:1024] if fwd else projT[1024:2048])
    normw = d[f"L0_attn_a_{pre}norm_w"].astype(np.float32)
    pb = d["L0_attn_a_proj_b"].astype(np.float32)
    pb_r = np.ascontiguousarray(pb.reshape(8, 128).T) if not fwd \
        else np.zeros((128, 8), np.float32)
    x0 = np.asarray(d["L0_attn_a_x"], np.float32)

    maps = []
    for core in range(8):
        g, qc = core // 4, core % 4
        t0 = qc * 2048
        xpad = np.zeros((2054, 1024), np.float32)
        lo, hi = t0 - 3, t0 + 2051
        slo, shi = max(lo, 0), min(hi, 8192)
        xpad[slo - lo:shi - lo] = x0[g, slo:shi]
        xh = np.ascontiguousarray(xpad.T)
        xres = np.ascontiguousarray(xcur[g, t0:t0 + 2048].T).astype(np.float32)
        alog8 = np.ascontiguousarray(alog[8 * qc:8 * qc + 8].reshape(8, 1))
        dvec = np.repeat(Dv[8 * qc:8 * qc + 8], 64).reshape(4, 128).T
        dvec = np.ascontiguousarray(dvec).astype(np.float32)
        wout512 = np.ascontiguousarray(wout[:, 512 * qc:512 * (qc + 1)])
        normw4 = np.ascontiguousarray(
            normw[512 * qc:512 * (qc + 1)].reshape(4, 128).T)
        hidx = np.zeros((128, 8), np.int32)
        for i in range(4):
            hidx[:, i] = (4 * qc + i) * 128 + np.arange(128)
            hidx[:, 4 + i] = (4 * qc + i) * 128 + np.arange(128)
        didx = (8 * qc + np.arange(8)).reshape(8, 1).astype(np.int32)
        m = dict(xh=xh, xres=xres, wiT=wiT, convw=convw, convb=convb, dtb=dtb,
                 alog8=alog8, dvecbc=dvec, wout512=wout512,
                 projTd=projTd, normw4=normw4, pb_r=pb_r,
                 hidx_d=hidx, didx_d=didx,
                 nwg_r=np.ascontiguousarray(
                     d["L0_attn_a_norm_weight"].astype(np.float32).reshape(8, 128).T))
        m.update(const_inputs())
        maps.append(m)
    return maps


def gather_stream(results, key="xo"):
    x = np.zeros((2, 8192, 1024), np.float32)
    for core in range(8):
        g, qc = core // 4, core % 4
        x[g, qc * 2048:(qc + 1) * 2048] = results[core][key].T
    return x


_BUILD_CACHE = {}


def _get(name, fn, *a):
    if name not in _BUILD_CACHE:
        _BUILD_CACHE[name] = fn(*a)
    return _BUILD_CACHE[name]


def kernel(**inputs):
    d = inputs
    x = np.asarray(d["L0_attn_a_x"], np.float32)
    cores = list(range(8))

    nc = _get("mamba_fwd", build_mamba, "fwd")
    res = run_bass_kernel_spmd(nc, stage_mamba(d, "fwd", x), core_ids=cores)
    x = gather_stream(res.results)
    nc = _get("mamba_bwd", build_mamba, "bwd")
    res = run_bass_kernel_spmd(nc, stage_mamba(d, "bwd", x), core_ids=cores)
    x = gather_stream(res.results)

    nc = _get("l0ffn", build_l0ffn)
    res = run_bass_kernel_spmd(nc, stage_l0ffn(d, x), core_ids=cores)
    x = gather_stream(res.results)

    nc = _get("l0attnb", build_l0attnb)
    res = run_bass_kernel_spmd(nc, stage_l0attnb(d, x), core_ids=cores)
    x = gather_stream(res.results)

    nc = _get("l0ffn", build_l0ffn)
    res = run_bass_kernel_spmd(nc, stage_l0ffn(d, x), core_ids=cores)
    x = gather_stream(res.results)

    nc = _get("l1attna", build_l1attna)
    res = run_bass_kernel_spmd(nc, stage_l1attna(d, x), core_ids=cores)
    x = gather_stream(res.results)

    nc = _get("l1ffn", build_l1ffn)
    res = run_bass_kernel_spmd(nc, stage_l1ffn(d, x), core_ids=cores)
    x = gather_stream(res.results)

    nc = _get("l1attnb", build_l1attnb)
    res = run_bass_kernel_spmd(nc, stage_l1attnb(d, x), core_ids=cores)
    x = gather_stream(res.results)

    nc = _get("l1ffn", build_l1ffn)
    res = run_bass_kernel_spmd(nc, stage_l1ffn(d, x), core_ids=cores)
    x = gather_stream(res.results)

    return x.astype(np.float32)
